# version 22
# baseline (speedup 1.0000x reference)
import numpy as np
import ml_dtypes
import concourse.bass as bass
import concourse.mybir as mybir
from concourse.bass_utils import run_bass_kernel_spmd

F32 = mybir.dt.float32
BF16 = mybir.dt.bfloat16
AX = mybir.AxisListType
ALU = mybir.AluOpType
ACT = mybir.ActivationFunctionType

C1 = float(np.float32(1.5 * 2.0 ** 23))
C2 = float(np.float32(1.5 * 2.0 ** 52))
SCALE_INV = float(np.float32(1.0 / np.sqrt(np.float32(1040.0))))
NORM_SLACK = float(np.float32(1.004 * 1.004))

D = 1024
S = 4096
T_OWN = 2048
NH = 16
HD = 64
INNER = 4096

ENGINES = ("tensor", "vector", "scalar", "sync", "gpsimd")


class Ev:
    __slots__ = ("eng", "dma", "needed", "sem", "val", "seq", "lane")

    def __init__(self, eng, dma):
        self.eng, self.dma, self.needed = eng, dma, False
        self.sem, self.val = None, None
        self.seq = 0
        self.lane = None


class Prog:

    def __init__(self, nc):
        self.nc = nc
        self.streams = {e: [] for e in ENGINES}
        self.seq = 0
        self.last_dma = {}

    def emit(self, eng, fn, after=(), dma=False, lane=None):
        ev = Ev(eng, dma)
        ev.seq = self.seq
        ev.lane = lane
        self.seq += 1
        deps = [a for a in after if a is not None]
        self.streams[eng].append((fn, deps, ev))
        return ev

    def dma(self, eng, out, in_, after=(), lane=None, **kw):
        ev = self.emit(eng, lambda e: e.dma_start(out=out, in_=in_, **kw),
                       after=after, dma=True, lane=lane)
        self.last_dma[(eng, lane)] = ev
        return ev

    def barrier(self):
        evs = [self.emit(e, _nopf(e),
                         after=[ev for (ee, ln), ev in self.last_dma.items()
                                if ee == e])
               for e in ENGINES]
        outs = []
        for e in ENGINES:
            deps = [ev for ev in evs if ev.eng != e]
            outs.append(self.emit(e, _nopf(e), after=deps))
        return outs

    def finalize(self):
        nc = self.nc
        for e in ENGINES:
            for fn, deps, ev in self.streams[e]:
                for a in deps:
                    a.needed = True
        sems = {}
        for e in ENGINES:
            sems[(e, False)] = nc.alloc_semaphore("c_" + e)
            sems[(e, True)] = nc.alloc_semaphore("d_" + e)
        lane_sems = {}
        for e in ENGINES:
            for fn, deps, ev in self.streams[e]:
                if ev.dma and ev.lane is not None and (e, ev.lane) not in lane_sems:
                    lane_sems[(e, ev.lane)] = nc.alloc_semaphore(
                        f"dl_{e}_{ev.lane}")
        issued_before = {}
        dma_seqs = {e: [] for e in ENGINES}
        lane_cnt = {}
        for e in ENGINES:
            cnt = {False: 0, True: 0}
            for fn, deps, ev in self.streams[e]:
                issued_before[id(ev)] = cnt[True]
                if ev.dma:
                    ev.needed = True
                if ev.needed and ev.dma and ev.lane is not None:
                    key = (e, ev.lane)
                    lane_cnt[key] = lane_cnt.get(key, 0) + 16
                    ev.sem = lane_sems[key]
                    ev.val = lane_cnt[key]
                elif ev.needed:
                    cnt[ev.dma] += 16 if ev.dma else 1
                    ev.sem = sems[(e, ev.dma)]
                    ev.val = cnt[ev.dma]
                    if ev.dma:
                        dma_seqs[e].append((ev.seq, ev.val))

        import bisect
        def q_total_at(qeng, seq):
            lst = dma_seqs[qeng]
            i = bisect.bisect_left(lst, (seq, -1))
            return lst[i - 1][1] if i else 0

        def run(eng_obj, e):
            for fn, deps, ev in self.streams[e]:
                waits = {}
                if ev.needed and not ev.dma and ev.val > 1:
                    waits[id(ev.sem)] = (ev.sem, ev.val - 1)
                for a in deps:
                    if a.eng == e and a.dma and a.lane is None:
                        sem, val = sems[(e, True)], issued_before[id(ev)]
                        if val == 0:
                            continue
                    elif a.dma and a.lane is None:
                        sem, val = a.sem, max(a.val, q_total_at(a.eng, ev.seq))
                    else:
                        sem, val = a.sem, a.val
                    key = id(sem)
                    if key not in waits or waits[key][1] < val:
                        waits[key] = (sem, val)
                for sem, val in waits.values():
                    eng_obj.wait_ge(sem, val)
                ins = fn(eng_obj)
                if ev.needed and ins is not None:
                    ins.then_inc(ev.sem, 16 if ev.dma else 1)

        with nc.Block() as block:
            @block.tensor
            def _(t):
                run(t, "tensor")

            @block.vector
            def _(v):
                run(v, "vector")

            @block.scalar
            def _(s):
                run(s, "scalar")

            @block.sync
            def _(y):
                run(y, "sync")

            @block.gpsimd
            def _(g):
                run(g, "gpsimd")


def _nopf(e):
    return lambda eng: eng.nop()


class Arena:

    def __init__(self, ap, total):
        self.ap, self.total, self.off = ap, total, 0

    def alloc(self, n):
        assert self.off + n <= self.total, f"arena overflow {self.off}+{n}>{self.total}"
        sl = self.ap[:, self.off:self.off + n]
        self.off += n
        return sl

    def mark(self):
        return self.off

    def reset(self, mark):
        self.off = mark



NF = 28500
NB = 45500


def build_nc():
    nc = bass.Bass("TRN2", target_bir_lowering=False, debug=False)

    def din(name, shape, dt=F32):
        return nc.dram_tensor(name, list(shape), dt, kind="ExternalInput").ap()

    x_all = din("x_all", (S, D))
    x_own = din("x_own", (T_OWN, D))
    wdq = din("wdq", (D, 512)); wdkv = din("wdkv", (D, 512))
    wuq = din("wuq", (512, D)); wqr = din("wqr", (512, D))
    wuk = din("wuk", (512, D)); wkr = din("wkr", (D, D))
    wuv = din("wuv", (512, D)); wo = din("wo", (D, D))
    wg = din("wg", (INNER, D)); wu = din("wu", (INNER, D)); wd = din("wd", (D, INNER))
    bdq_c = din("bdq_c", (128, 4)); bdkv_c = din("bdkv_c", (128, 4))
    buq_c = din("buq_c", (128, 8)); bqr_c = din("bqr_c", (128, 8))
    buk_c = din("buk_c", (128, 8)); bkr_c = din("bkr_c", (128, 8))
    bvb_r = din("bvb_r", (1, D)); bwo_r = din("bwo_r", (1, D))
    identf_d = din("identf", (128, 128))
    identb_d = din("identb", (128, 128), BF16)
    ind_d = din("ind_norm", (D, NH))
    sel_d = din("sel_pairs", (NH, 1024))
    id16_d = din("id16", (NH, NH))
    ones16_d = din("ones16", (NH, 128))
    onesc_d = din("onescol", (1, 128))
    cosq_d = din("cos_q", (512, T_OWN)); sinq_d = din("sin_q", (512, T_OWN))
    cosk_d = din("cos_k", (512, S)); sink_d = din("sin_k", (512, S))
    locq_d = din("lo_cq", (1, T_OWN)); losq_d = din("lo_sq", (1, T_OWN))
    lock_d = din("lo_ck", (1, S)); losk_d = din("lo_sk", (1, S))

    out_d = nc.dram_tensor("out", [T_OWN, D], F32, kind="ExternalOutput").ap()

    def scr(name, shape, dt):
        return nc.dram_tensor(name, list(shape), dt, kind="Internal").ap()

    lnTk_hi = scr("lnTk_hi", (D, S), BF16); lnTk_lo = scr("lnTk_lo", (D, S), BF16)
    ctq_hi = scr("ctq_hi", (512, T_OWN), BF16); ctq_lo = scr("ctq_lo", (512, T_OWN), BF16)
    ctkv_hi = scr("ctkv_hi", (512, S), BF16); ctkv_lo = scr("ctkv_lo", (512, S), BF16)
    qs_hi = scr("qs_hi", (D, T_OWN), BF16); qs_lo = scr("qs_lo", (D, T_OWN), BF16)
    ks_hi = scr("ks_hi", (D, S), BF16); ks_lo = scr("ks_lo", (D, S), BF16)
    vp_s = scr("vp", (S, 1040), F32)
    wgq_s = scr("wgq", (INNER, D), BF16)
    wuq_s2 = scr("wuq_q", (INNER, D), BF16)
    wdq_s2 = scr("wdq_q", (D, INNER), BF16)
    wgT_s = scr("wgT", (D, INNER), BF16)
    wuT_s = scr("wuT", (D, INNER), BF16)
    wdT_s = scr("wdT", (INNER, D), BF16)
    x2_s = scr("x2", (T_OWN, D), F32)
    xq2T_s = scr("xq2T", (T_OWN, INNER), BF16)
    attnU_hi_s = scr("attnU_hi", (D, T_OWN), BF16)
    attnU_lo_s = scr("attnU_lo", (D, T_OWN), BF16)

    PF = nc.alloc_sbuf_tensor("poolF", [128, NF], F32).ap()
    PB = nc.alloc_sbuf_tensor("poolB", [128, NB], BF16).ap()
    p = [nc.alloc_psum_tensor(f"p{i}", [128, 512], F32).ap() for i in range(7)]
    pb7 = nc.alloc_psum_tensor("pb7", [128, 1024], BF16).ap()

    af = Arena(PF, NF)
    ab = Arena(PB, NB)

    identf = af.alloc(128)
    stats = af.alloc(64)
    cols = af.alloc(64)
    nrm_q = af.alloc(512)
    nrm_k = af.alloc(512)
    m_cols = af.alloc(16)
    ind_sb = af.alloc(128)
    sel_sb = af.alloc(1024)
    id16_sb = af.alloc(16)
    ones16_sb = af.alloc(128)
    onesc_sb = af.alloc(128)
    bvb_bc = af.alloc(1024)
    bwo_bc = af.alloc(1024)
    biasc = af.alloc(40)
    mgate_c = af.alloc(1)
    mup_c = af.alloc(1)
    mdown_c = af.alloc(1)
    zmask_c = af.alloc(1)
    onecol_c = af.alloc(1)
    scinv_c = af.alloc(1)
    sx_all = af.alloc(16)
    ev1_all = af.alloc(16)
    sg_all = af.alloc(16)
    su_all = af.alloc(16)
    sd_all = af.alloc(16)
    PERSIST_F = af.mark()

    identb = ab.alloc(128)
    PERSIST_B = ab.mark()

    P = Prog(nc)

    ev_c = []
    ev_c.append(P.dma("sync", identf[:, :], identf_d[:, :]))
    ev_c.append(P.dma("sync", identb[:, :], identb_d[:, :]))
    ev_c.append(P.dma("sync", ind_sb.rearrange("p (c h) -> p c h", h=NH),
                     ind_d[:, :].rearrange("(c p) h -> p c h", p=128)))
    ev_c.append(P.dma("sync", sel_sb[0:NH, :], sel_d[:, :]))
    ev_c.append(P.dma("sync", id16_sb[0:NH, :], id16_d[:, :]))
    ev_c.append(P.dma("sync", ones16_sb[0:NH, :], ones16_d[:, :]))
    ev_c.append(P.dma("sync", onesc_sb[0:1, :], onesc_d[:, :]))
    bias_dmas = []
    for dst_off, src in ((0, bdq_c), (4, bdkv_c), (8, buq_c), (16, bqr_c),
                         (24, buk_c), (32, bkr_c)):
        n = src.shape[1]
        bias_dmas.append(P.dma("sync", biasc[:, dst_off:dst_off + n], src[:, :]))
    brow = PF[:, PERSIST_F:PERSIST_F + 2048]
    ev_b1 = P.dma("sync", brow[0:1, 0:D], bvb_r[:, :])
    ev_b2 = P.dma("sync", brow[0:1, 1024:1024 + D], bwo_r[:, :])
    bc_ev = []
    for j, (src_off, dst) in enumerate(((0, bvb_bc), (1024, bwo_bc))):
        for half in range(2):
            e1 = P.emit("tensor", (lambda so=src_off, hf=half: lambda t: t.matmul(
                p[6][:, 0:512], onesc_sb[0:1, 0:128],
                brow[0:1, so + 512 * hf:so + 512 * hf + 512],
                start=True, stop=True))(),
                after=[ev_b1, ev_b2, ev_c[6]] + bc_ev[-1:])
            e2 = P.emit("vector", (lambda d=dst, hf=half: lambda v: v.tensor_copy(
                d[:, 512 * hf:512 * hf + 512], p[6][:, 0:512]))(), after=[e1])
            bc_ev.append(e2)
    ev_one = P.emit("vector", lambda v: v.memset(onecol_c[:, 0:1], 1.0))
    ev_za = P.emit("gpsimd", lambda g: g.tensor_scalar(
        cols[:, 0:1], onecol_c[:, 0:1], C1, C1, ALU.add, ALU.subtract), after=[ev_one])
    ev_zb = P.emit("gpsimd", lambda g: g.tensor_scalar(
        cols[:, 1:2], onecol_c[:, 0:1], C2, C2, ALU.add, ALU.subtract), after=[ev_one])
    ev_zm = P.emit("vector", lambda v: v.tensor_tensor(
        zmask_c[:, 0:1], cols[:, 0:1], cols[:, 1:2], ALU.subtract), after=[ev_za, ev_zb])
    ev_s0 = P.emit("vector", lambda v: v.memset(cols[:, 45:46], 1040.0))
    ev_s1 = P.emit("scalar", lambda s: s.activation(
        out=cols[:, 46:47], in_=cols[:, 45:46], func=ACT.Sqrt), after=[ev_s0])
    ev_s2 = P.emit("vector", lambda v: v.reciprocal(cols[:, 47:48], cols[:, 46:47]),
                   after=[ev_s1])
    ev_s3 = P.emit("gpsimd", lambda g: g.memset(cols[:, 48:49], SCALE_INV))
    ev_s4 = P.emit("gpsimd", lambda g: g.tensor_tensor(
        cols[:, 49:50], cols[:, 48:49], cols[:, 47:48], ALU.subtract),
        after=[ev_s2, ev_s3])
    ev_s5 = P.emit("vector", lambda v: v.tensor_single_scalar(
        cols[:, 50:51], cols[:, 49:50], zmask_c[:, 0:1], ALU.mult),
        after=[ev_s4, ev_zm])
    ev_sc = P.emit("gpsimd", lambda g: g.tensor_tensor(
        scinv_c[:, 0:1], cols[:, 47:48], cols[:, 50:51], ALU.add),
        after=[ev_s5])
    init_evs = ev_c + bias_dmas + bc_ev + [ev_zm, ev_sc]

    _sq_chain = []
    _sp_chain = []

    def E(eng, after, op, *args, **kw):
        return P.emit(eng, lambda e, _o=op, _a=args, _k=kw: getattr(e, _o)(*_a, **_k),
                      after=after)

    def prep_w_hilo(wdram, csel, cpart, stage, dst_hi, dst_lo, after):
        row0, nch = csel
        ocols = wdram.shape[1]
        n = nch * ocols
        src = wdram[row0 * 128:(row0 + nch) * 128, :].rearrange(
            "(c p) o -> p c o", p=128)
        ev_l = P.dma("sync",
                     stage[:, 0:n].rearrange("p (c o) -> p c o", o=ocols),
                     src, after=after)
        ev_h = E("vector", [ev_l], "tensor_copy", dst_hi[:, 0:n], stage[:, 0:n])
        ev_o = E("vector", [ev_h], "tensor_tensor", dst_lo[:, 0:n], stage[:, 0:n],
                 dst_hi[:, 0:n], ALU.subtract)
        return ev_o

    def ln_batch(src, dst, nrc, sqd, after, after_apply=()):
        ev_s, ev_q = [], []
        for rc in range(nrc):
            sl = src[:, 1024 * rc:1024 * rc + 1024]
            ev_s.append(E("vector", after, "reduce_sum",
                          stats[:, rc:rc + 1], sl, axis=AX.X))
            ev_q.append(E("scalar", list(after) + _sq_chain[:], "activation",
                          out=sqd[:, 0:1024], in_=sl, func=ACT.Square,
                          accum_out=stats[:, 4 + rc:5 + rc]))
            _sq_chain[:] = [ev_q[-1]]
        ev_mu, ev_t = [], []
        for rc in range(nrc):
            ev_mu.append(E("vector", [ev_s[rc]], "tensor_scalar_mul",
                           stats[:, 8 + rc:9 + rc], stats[:, rc:rc + 1], 1.0 / 1024))
        for rc in range(nrc):
            ev_t.append(E("vector", [ev_q[rc]], "tensor_scalar",
                          stats[:, 12 + rc:13 + rc], stats[:, 4 + rc:5 + rc],
                          1.0 / 1024, 1e-5, ALU.mult, ALU.add))
        ev_m2 = []
        for rc in range(nrc):
            ev_m2.append(E("vector", [ev_mu[rc]], "tensor_tensor",
                           stats[:, 16 + rc:17 + rc], stats[:, 8 + rc:9 + rc],
                           stats[:, 8 + rc:9 + rc], ALU.mult))
        ev_v = []
        for rc in range(nrc):
            ev_v.append(E("vector", [ev_t[rc], ev_m2[rc]], "tensor_tensor",
                          stats[:, 20 + rc:21 + rc], stats[:, 12 + rc:13 + rc],
                          stats[:, 16 + rc:17 + rc], ALU.subtract))
        ev_sd = []
        for rc in range(nrc):
            ev_sd.append(E("scalar", [ev_v[rc]], "activation",
                           out=stats[:, 24 + rc:25 + rc],
                           in_=stats[:, 20 + rc:21 + rc], func=ACT.Sqrt))
        ev_r = []
        for rc in range(nrc):
            ev_r.append(E("vector", [ev_sd[rc]], "reciprocal",
                          stats[:, 28 + rc:29 + rc], stats[:, 24 + rc:25 + rc]))
        ev_a = []
        for rc in range(nrc):
            ev_a.append(E("vector", list(after_apply) + [ev_r[rc]], "tensor_scalar",
                          dst[:, 1024 * rc:1024 * rc + 1024],
                          src[:, 1024 * rc:1024 * rc + 1024],
                          stats[:, 8 + rc:9 + rc], stats[:, 28 + rc:29 + rc],
                          ALU.subtract, ALU.mult))
        return ev_a

    def transpose_hilo(src, dst_hi, dst_lo, nrc, ev_src, ev_free):
        evs = []
        prev_evac = {0: list(ev_free), 1: list(ev_free)}
        for fc in range(8):
            bank = p[fc % 2]
            tevs = []
            for rc in range(nrc):
                tevs.append(E("tensor",
                              [ev_src[rc]] + prev_evac[fc % 2],
                              "transpose",
                              bank[:, 128 * rc:128 * rc + 128],
                              src[:, 1024 * rc + 128 * fc:1024 * rc + 128 * fc + 128],
                              identf[:, 0:128]))
            dh = dst_hi[:, 512 * fc:512 * fc + 128 * nrc]
            dl = dst_lo[:, 512 * fc:512 * fc + 128 * nrc]
            pv = bank[:, 0:128 * nrc]
            e_h = E("scalar", [tevs[-1]], "activation",
                    out=dh, in_=pv, func=ACT.Copy)
            e_l = E("vector", [e_h], "tensor_tensor", dl, pv, dh, ALU.subtract)
            prev_evac[fc % 2] = [e_l]
            evs.append(e_l)
        return evs

    def mm3_group(bank, lhs_hi, lhs_lo, lhs_sl, rhs_hi, rhs_lo, rhs_sl, nfc,
                  after, tile_position=None, after_fc=None):
        ev = None
        first = True
        for fc in range(nfc):
            wsl = lhs_sl(fc)
            xsl = rhs_sl(fc)
            dep = list(after) if first else []
            if after_fc is not None:
                dep = dep + after_fc(fc)
            for pi, (wop, xop) in enumerate(((lhs_hi, rhs_hi), (lhs_hi, rhs_lo),
                                             (lhs_lo, rhs_hi))):
                last = (fc == nfc - 1 and pi == 2)
                ev = E("tensor", dep, "matmul",
                       bank, wop[:, wsl[0]:wsl[1]], xop[:, xsl[0]:xsl[1]],
                       start=first, stop=last,
                       **({"tile_position": tile_position} if tile_position else {}))
                dep = []
                first = False
        return ev

    af.reset(PERSIST_F)
    ab.reset(PERSIST_B)
    af.alloc(2048)
    wstage = af.alloc(4096)
    x_sb = af.alloc(4096)
    ln_sb = af.alloc(4096)
    sqd = af.alloc(1024)
    ct_f32 = af.alloc(2048)
    v_f32 = wstage

    wdq_hi = ab.alloc(4096); wdq_lo = ab.alloc(4096)
    wdkv_hi = ab.alloc(4096); wdkv_lo = ab.alloc(4096)
    wuv_hi = ab.alloc(4096); wuv_lo = ab.alloc(4096)
    lnT_hi = ab.alloc(4096); lnT_lo = ab.alloc(4096)
    ct_hi = ab.alloc(2048); ct_lo = ab.alloc(2048)
    ctk_hi = ab.alloc(2048); ctk_lo = ab.alloc(2048)

    ev = prep_w_hilo(wdq, (0, 8), 128, wstage, wdq_hi, wdq_lo, init_evs)
    ev = prep_w_hilo(wdkv, (0, 8), 128, wstage, wdkv_hi, wdkv_lo, [ev])
    ev_wprep = prep_w_hilo(wuv, (0, 4), 128, wstage, wuv_hi, wuv_lo, [ev])

    def emit_a1(src_x, ntiles, lnTh_s, lnTl_s, cth_s, ctl_s, w_hi, w_lo,
                bias_o, do_kv, start_evs, spill_lnT=True):
        prev_lnfree = list(start_evs)
        prev_xfree = list(start_evs)
        prev_ctfree = list(start_evs)
        prev_vfree = list(start_evs)
        prev_lnT_readers = list(start_evs)
        bank_free = {i: list(start_evs) for i in range(2, 6)}
        for st in range(ntiles):
            e_x = P.dma("sync",
                        x_sb.rearrange("p (c f) -> p c f", f=1024),
                        src_x[512 * st:512 * st + 512, :].rearrange(
                            "(c p) f -> p c f", p=128),
                        after=prev_xfree)
            ev_app = ln_batch(x_sb, ln_sb, 4, sqd, [e_x], after_apply=prev_lnfree)
            prev_xfree = list(ev_app)
            ev_tr = transpose_hilo(ln_sb, lnT_hi, lnT_lo, 4, ev_app,
                                   prev_lnT_readers)
            prev_lnfree = ev_tr[-8:]
            spill_evs = []
            if spill_lnT:
                e_s1 = P.dma("gpsimd", lnTh_s[:, 512 * st:512 * st + 512].rearrange(
                    "(c p) t -> p c t", p=128),
                    lnT_hi[:, 0:4096].rearrange("p (c t) -> p c t", t=512),
                    after=ev_tr, lane="a_lnt")
                e_s2 = P.dma("gpsimd", lnTl_s[:, 512 * st:512 * st + 512].rearrange(
                    "(c p) t -> p c t", p=128),
                    lnT_lo[:, 0:4096].rearrange("p (c t) -> p c t", t=512),
                    after=ev_tr, lane="a_lnt")
                spill_evs = [e_s1, e_s2]
                lnT_spill_evs = [e_s1, e_s2]
            if not spill_lnT:
                lnT_spill_evs = []
            if do_kv:
                groups = [(wdkv_hi, wdkv_lo, 4 + 0, ctk_hi, ctk_lo, cth_s, ctl_s)]
            else:
                groups = [(w_hi, w_lo, bias_o, ct_hi, ct_lo, cth_s, ctl_s)]
            ct_evs = []
            lnT_reader_evs = []
            for (wh, wl, boff, chid, clod, chs, cls) in groups:
                for oc in range(4):
                    bank = p[2 + oc % 4]
                    evm = mm3_group(
                        bank[:, 0:512], wh, wl,
                        lambda fc, oc=oc: (512 * fc + 128 * oc, 512 * fc + 128 * oc + 128),
                        lnT_hi, lnT_lo,
                        lambda fc: (512 * fc, 512 * fc + 512),
                        8, bank_free[2 + oc % 4],
                        after_fc=lambda fc: [ev_tr[fc]])
                    e_ev = E("scalar", [evm] + prev_ctfree, "activation",
                             out=ct_f32[:, 512 * oc:512 * oc + 512],
                             in_=bank[:, 0:512], func=ACT.Identity,
                             bias=biasc[:, boff + oc:boff + oc + 1])
                    lnT_reader_evs.append(evm)
                    e_h = E("vector", [e_ev] + prev_ctfree, "tensor_copy",
                            chid[:, 512 * oc:512 * oc + 512],
                            ct_f32[:, 512 * oc:512 * oc + 512])
                    e_l = E("vector", [e_h], "tensor_tensor",
                            clod[:, 512 * oc:512 * oc + 512],
                            ct_f32[:, 512 * oc:512 * oc + 512],
                            chid[:, 512 * oc:512 * oc + 512], ALU.subtract)
                    bank_free[2 + oc % 4] = [e_ev]
                    ct_evs.append(e_l)
                e_c1 = P.dma("gpsimd", chs[:, 512 * st:512 * st + 512].rearrange(
                    "(c p) t -> p c t", p=128),
                    chid[:, 0:2048].rearrange("p (c t) -> p c t", t=512),
                    after=ct_evs, lane="a_ct")
                e_c2 = P.dma("gpsimd", cls[:, 512 * st:512 * st + 512].rearrange(
                    "(c p) t -> p c t", p=128),
                    clod[:, 0:2048].rearrange("p (c t) -> p c t", t=512),
                    after=ct_evs, lane="a_ct")
                spill_evs += [e_c1, e_c2]
                lnT_reader_evs += [e_c1, e_c2]
                prev_ctfree = [e_c2]
            prev_lnT_readers = lnT_reader_evs + lnT_spill_evs
            if do_kv:
                v_evs = []
                last_v_mm = None
                for rc in range(4):
                    for ot in range(2):
                        bank = p[2 + (2 * rc + ot) % 4]
                        evm = mm3_group(
                            bank[:, 0:512], ctk_hi, ctk_lo,
                            lambda fc, rc=rc: (512 * fc + 128 * rc,
                                               512 * fc + 128 * rc + 128),
                            wuv_hi, wuv_lo,
                            lambda fc, ot=ot: (1024 * fc + 512 * ot,
                                               1024 * fc + 512 * ot + 512),
                            4, bank_free[2 + (2 * rc + ot) % 4],
                            after_fc=lambda fc: [ct_evs[fc]])
                        e_v = E("vector", [evm] + prev_vfree, "tensor_tensor",
                                v_f32[:, 1024 * rc + 512 * ot:1024 * rc + 512 * ot + 512],
                                bank[:, 0:512],
                                bvb_bc[:, 512 * ot:512 * ot + 512], ALU.add)
                        bank_free[2 + (2 * rc + ot) % 4] = [e_v]
                        last_v_mm = evm
                        v_evs.append(e_v)
                vsp = []
                for rc in range(4):
                    vdst = vp_s[512 * st + 128 * rc:512 * st + 128 * rc + 128,
                                :].rearrange("p (g u) -> p g u", u=65)[:, :, 0:64]
                    vsrc = v_f32[:, 1024 * rc:1024 * rc + 1024].rearrange(
                        "p (g w) -> p g w", w=64)
                    vsp.append(P.dma("gpsimd", vdst, vsrc, after=v_evs,
                                     lane="a_v"))
                prev_vfree = vsp[-1:]
                prev_ctfree = prev_ctfree + [last_v_mm]
                spill_evs += vsp
            prev_xfree = list(ev_app)
        return spill_evs

    a1k_done = emit_a1(x_all, 8, lnTk_hi, lnTk_lo, ctkv_hi, ctkv_lo,
                       None, None, 4, True, [ev_wprep])
    a1q_done = emit_a1(x_own, 4, None, None, ctq_hi, ctq_lo,
                       wdq_hi, wdq_lo, 0, False, list(a1k_done),
                       spill_lnT=False)
    P.barrier()

    def emit_a2(ntiles, ct_h_s, ct_l_s, use_lnT, lnh_s, lnl_s,
                wA_hi, wA_lo, boffA, wB_hi, wB_lo, boffB, nfcB,
                cos_d, sin_d, loc_d, los_d, dst_hi_s, dst_lo_s, nrm,
                ctbuf_hi, ctbuf_lo, lnbuf_hi, lnbuf_lo,
                pc_f32, pr_f32, cos_sb, sin_sb, tmpA, tmpB, tmpLo, sqbuf,
                out_hi, out_lo, start_evs, tmpG1=None, tmpG2=None):
        prev_ct_readers = list(start_evs)
        prev_ln_readers = list(start_evs)
        prev_pc_readers = list(start_evs)
        prev_out_readers = list(start_evs)
        bank_free = {i: list(start_evs) for i in range(2, 6)}
        nrm_ev = [None]
        for st in range(ntiles):
            loads = []
            e1 = P.dma("sync", ctbuf_hi.rearrange("p (c t) -> p c t", t=512),
                       ct_h_s[:, 512 * st:512 * st + 512].rearrange(
                           "(c p) t -> p c t", p=128), after=prev_ct_readers)
            e2 = P.dma("sync", ctbuf_lo.rearrange("p (c t) -> p c t", t=512),
                       ct_l_s[:, 512 * st:512 * st + 512].rearrange(
                           "(c p) t -> p c t", p=128), after=prev_ct_readers)
            loads += [e1, e2]
            if use_lnT:
                e3 = P.dma("sync", lnbuf_hi.rearrange("p (c t) -> p c t", t=512),
                           lnh_s[:, 512 * st:512 * st + 512].rearrange(
                               "(c p) t -> p c t", p=128), after=prev_ln_readers)
                e4 = P.dma("sync", lnbuf_lo.rearrange("p (c t) -> p c t", t=512),
                           lnl_s[:, 512 * st:512 * st + 512].rearrange(
                               "(c p) t -> p c t", p=128), after=prev_ln_readers)
                loads += [e3, e4]
            ec = P.dma("sync", cos_sb.rearrange("p (c t) -> p c t", t=512),
                       cos_d[:, 512 * st:512 * st + 512].rearrange(
                           "(c p) t -> p c t", p=128), after=prev_pc_readers)
            es = P.dma("sync", sin_sb.rearrange("p (c t) -> p c t", t=512),
                       sin_d[:, 512 * st:512 * st + 512].rearrange(
                           "(c p) t -> p c t", p=128), after=prev_pc_readers)
            el1 = P.dma("sync", tmpLo[0:1, 0:512],
                        loc_d[0:1, 512 * st:512 * st + 512], after=prev_pc_readers)
            el2 = P.dma("sync", tmpLo[0:1, 512:1024],
                        los_d[0:1, 512 * st:512 * st + 512], after=prev_pc_readers)
            proj_evs = {"c": [None] * 8, "r": [None] * 8}
            reader_evs = []
            rope_evs = []
            order = []
            for c4 in range(4):
                order += [("c", c4), ("c", c4 + 4), ("r", c4), ("r", c4 + 4)]
            specs = {"c": (wA_hi, wA_lo, boffA, ctbuf_hi, ctbuf_lo, 4, pc_f32),
                     "r": (wB_hi, wB_lo, boffB,
                           lnbuf_hi if use_lnT else ctbuf_hi,
                           lnbuf_lo if use_lnT else ctbuf_lo, nfcB, pr_f32)}
            for gi, (which, oc) in enumerate(order):
                whi, wlo, boff, rhi, rlo, nfc, dstb = specs[which]
                bank = p[2 + gi % 4]
                evm = mm3_group(
                    bank[:, 0:512], whi, wlo,
                    lambda fc, oc=oc: (1024 * fc + 128 * oc,
                                       1024 * fc + 128 * oc + 128),
                    rhi, rlo,
                    lambda fc: (512 * fc, 512 * fc + 512),
                    nfc, loads + bank_free[2 + gi % 4])
                e_ev = E("scalar", [evm] + (prev_pc_readers if st > 0 else []),
                         "activation",
                         out=dstb[:, 512 * oc:512 * oc + 512],
                         in_=bank[:, 0:512], func=ACT.Identity,
                         bias=biasc[:, boff + oc:boff + oc + 1])
                bank_free[2 + gi % 4] = [e_ev]
                reader_evs.append(evm)
                proj_evs[which][oc] = e_ev
                if which == "r" and oc >= 4:
                    c4 = oc - 4
                    eng2 = "vector" if c4 % 2 == 0 else "gpsimd"
                    x0 = pr_f32[:, 512 * c4:512 * c4 + 512]
                    x1 = pr_f32[:, 2048 + 512 * c4:2048 + 512 * c4 + 512]
                    cc = cos_sb[:, 512 * c4:512 * c4 + 512]
                    sc = sin_sb[:, 512 * c4:512 * c4 + 512]
                    q0 = pc_f32[:, 512 * c4:512 * c4 + 512]
                    q1 = pc_f32[:, 2048 + 512 * c4:2048 + 512 * c4 + 512]
                    tA = tmpA if c4 % 2 == 0 else tmpG1
                    tB = tmpB if c4 % 2 == 0 else tmpG2
                    dep = [proj_evs["c"][c4], proj_evs["c"][c4 + 4],
                           proj_evs["r"][c4], proj_evs["r"][c4 + 4], ec, es]
                    ea = E(eng2, dep, "tensor_tensor", tA, x0, cc, ALU.mult)
                    eb = E(eng2, [ea], "tensor_tensor", tB, x1, sc, ALU.mult)
                    e0 = E(eng2, [ea, eb], "tensor_tensor", tA, tA, tB,
                           ALU.subtract)
                    e0b = E(eng2, [e0], "tensor_tensor", q0, q0, tA, ALU.add)
                    e2a = E(eng2, [e0], "tensor_tensor", tB, x1, cc, ALU.mult)
                    e2b = E(eng2, [e0b], "tensor_tensor", tA, x0, sc, ALU.mult)
                    e1c = E(eng2, [e2a, e2b], "tensor_tensor", tB, tB, tA,
                            ALU.add)
                    e1d = E(eng2, [e1c], "tensor_tensor", q1, q1, tB, ALU.add)
                    rope_evs += [e0b, e1d]
            prev_ct_readers = [reader_evs[-1]]
            prev_ln_readers = [reader_evs[-1]]
            x0r = pr_f32[0:1, 0:512]
            x1r = pr_f32[0:1, 2048:2560]
            lc = tmpLo[0:1, 0:512]
            ls = tmpLo[0:1, 512:1024]
            ca = E("vector", [el1, el2] + rope_evs, "tensor_tensor",
                   tmpA[0:1, :], x0r, lc, ALU.mult)
            cb = E("vector", [ca], "tensor_tensor", tmpB[0:1, :], x1r, ls, ALU.mult)
            cc1 = E("vector", [ca, cb], "tensor_tensor", tmpA[0:1, :], tmpA[0:1, :],
                    tmpB[0:1, :], ALU.subtract)
            cd = E("vector", [cc1], "tensor_tensor", pc_f32[0:1, 0:512],
                   pc_f32[0:1, 0:512], tmpA[0:1, :], ALU.add)
            ce = E("vector", [cc1, cd], "tensor_tensor", tmpA[0:1, :], x1r, lc,
                   ALU.mult)
            cf = E("vector", [ce], "tensor_tensor", tmpB[0:1, :], x0r, ls, ALU.mult)
            cg = E("vector", [ce, cf], "tensor_tensor", tmpA[0:1, :], tmpA[0:1, :],
                   tmpB[0:1, :], ALU.add)
            ch2 = E("vector", [cg], "tensor_tensor", pc_f32[0:1, 2048:2560],
                    pc_f32[0:1, 2048:2560], tmpA[0:1, :], ALU.add)
            asm_evs = rope_evs + [cd, ch2]
            e_hi = E("vector", asm_evs + prev_out_readers, "tensor_copy",
                     out_hi[:, 0:4096], pc_f32[:, 0:4096])
            e_lo = E("vector", [e_hi], "tensor_tensor", out_lo[:, 0:4096],
                     pc_f32[:, 0:4096], out_hi[:, 0:4096], ALU.subtract)
            prev_pc_readers = [e_hi, e_lo]
            sq_evs = []
            nev = None
            for fc in range(8):
                half = sqbuf[:, 512 * (fc % 2):512 * (fc % 2) + 512]
                e_sq = E("scalar", [e_hi] + ([sq_evs[-2]] if fc >= 2 else [])
                         + _sq_chain[:],
                         "activation", out=half,
                         in_=out_hi[:, 512 * fc:512 * fc + 512], func=ACT.Square)
                _sq_chain[:] = [e_sq]
                nev = E("tensor", [e_sq] + ([nrm_ev[0]] if fc == 0 and nrm_ev[0] else []),
                        "matmul", p[6][0:NH, 0:512],
                        ind_sb[:, NH * fc:NH * fc + NH], half,
                        start=(fc == 0), stop=(fc == 7))
                sq_evs.append(nev)
            if st == 0:
                e_n = E("vector", [nev], "tensor_copy", nrm[0:NH, 0:512],
                        p[6][0:NH, 0:512])
            else:
                e_n = E("vector", [nev], "tensor_tensor", nrm[0:NH, 0:512],
                        nrm[0:NH, 0:512], p[6][0:NH, 0:512], ALU.max)
            nrm_ev[0] = e_n
            sp = []
            for buf, dscr in ((out_hi, dst_hi_s), (out_lo, dst_lo_s)):
                dst = dscr[:, 512 * st:512 * st + 512].rearrange(
                    "(c p) t -> p c t", p=128)
                srcv = buf[:, 0:4096].rearrange("p (c t) -> p c t", t=512)
                sp.append(P.dma("gpsimd", dst, srcv, after=[e_hi, e_lo],
                                lane="a_qk"))
            prev_out_readers = sp[-1:] + [sq_evs[-1]]
        return sp + [nrm_ev[0]]

    af.reset(PERSIST_F)
    ab.reset(PERSIST_B)
    wstage2 = af.alloc(4096)
    pc_f32 = af.alloc(4096)
    pr_f32 = af.alloc(4096)
    cos_sb = af.alloc(2048)
    sin_sb = af.alloc(2048)
    tmpA2 = af.alloc(512)
    tmpB2 = af.alloc(512)
    tmpG1a = af.alloc(512)
    tmpG2a = af.alloc(512)
    tmpLo2 = af.alloc(1024)
    sqbuf2 = af.alloc(1024)

    wA_hi = ab.alloc(4096); wA_lo = ab.alloc(4096)
    wB_hi = ab.alloc(8192); wB_lo = ab.alloc(8192)
    ctb_hi = ab.alloc(2048); ctb_lo = ab.alloc(2048)
    lnb_hi = ab.alloc(4096); lnb_lo = ab.alloc(4096)
    o_hi = ab.alloc(4096); o_lo = ab.alloc(4096)

    ev = prep_w_hilo(wuq, (0, 4), 128, wstage2, wA_hi, wA_lo, list(a1q_done))
    ev = prep_w_hilo(wqr, (0, 4), 128, wstage2, wB_hi, wB_lo, [ev])
    a2a_done = emit_a2(4, ctq_hi, ctq_lo, False, None, None,
                       wA_hi, wA_lo, 8, wB_hi, wB_lo, 16, 4,
                       cosq_d, sinq_d, locq_d, losq_d, qs_hi, qs_lo, nrm_q,
                       ctb_hi, ctb_lo, lnb_hi, lnb_lo,
                       pc_f32, pr_f32, cos_sb, sin_sb, tmpA2, tmpB2, tmpLo2,
                       sqbuf2, o_hi, o_lo, [ev], tmpG1=tmpG1a, tmpG2=tmpG2a)

    ev = prep_w_hilo(wuk, (0, 4), 128, wstage2, wA_hi, wA_lo, list(a2a_done))
    ev = prep_w_hilo(wkr, (0, 4), 128, wstage2, wB_hi, wB_lo, [ev])
    ev = prep_w_hilo(wkr, (4, 4), 128, wstage2,
                     wB_hi[:, 4096:8192], wB_lo[:, 4096:8192], [ev])
    a2b_done = emit_a2(8, ctkv_hi, ctkv_lo, True, lnTk_hi, lnTk_lo,
                       wA_hi, wA_lo, 24, wB_hi, wB_lo, 32, 8,
                       cosk_d, sink_d, lock_d, losk_d, ks_hi, ks_lo, nrm_k,
                       ctb_hi, ctb_lo, lnb_hi, lnb_lo,
                       pc_f32, pr_f32, cos_sb, sin_sb, tmpA2, tmpB2, tmpLo2,
                       sqbuf2, o_hi, o_lo, [ev], tmpG1=tmpG1a, tmpG2=tmpG2a)
    P.barrier()

    e_mq = E("vector", a2b_done, "reduce_max", cols[0:NH, 2:3],
             nrm_q[0:NH, 0:512], axis=AX.X)
    e_mk = E("vector", a2b_done, "reduce_max", cols[0:NH, 3:4],
             nrm_k[0:NH, 0:512], axis=AX.X)
    e_sq1 = E("scalar", [e_mq], "activation", out=cols[0:NH, 4:5],
              in_=cols[0:NH, 2:3], func=ACT.Sqrt)
    e_sq2 = E("scalar", [e_mk], "activation", out=cols[0:NH, 5:6],
              in_=cols[0:NH, 3:4], func=ACT.Sqrt)
    e_bnd = E("vector", [e_sq1, e_sq2], "tensor_tensor", cols[0:NH, 6:7],
              cols[0:NH, 4:5], cols[0:NH, 5:6], ALU.mult)
    e_mneg = E("gpsimd", [e_bnd], "tensor_scalar_mul", cols[0:NH, 7:8],
               cols[0:NH, 6:7], -NORM_SLACK * SCALE_INV)
    e_diag = E("vector", [e_mneg], "tensor_scalar", cols[0:NH, 8:24],
               id16_sb[0:NH, :], cols[0:NH, 7:8], None, ALU.mult)
    e_mmm = E("tensor", [e_diag], "matmul", p[6][:, 0:16],
              ones16_sb[0:NH, :], cols[:, 8:24][0:NH, :], start=True, stop=True)
    e_mcols = E("vector", [e_mmm], "tensor_copy", m_cols[:, 0:16], p[6][:, 0:16])

    af.reset(PERSIST_F)
    ab.reset(PERSIST_B)
    v_sb = af.alloc(4160)
    aT = [af.alloc(512), af.alloc(512), af.alloc(512), af.alloc(512)]
    anf_b = [af.alloc(512) for _ in range(4)]
    prodb = af.alloc(512)
    dstage = af.alloc(2048)

    qh_sb = ab.alloc(2048); ql_sb = ab.alloc(2048)
    kh_sb = ab.alloc(4096); kl_sb = ab.alloc(4096)
    au_hi = ab.alloc(2048); au_lo = ab.alloc(2048)

    wtile0 = af.alloc(4096)
    qa = af.alloc(4096)
    qb = ab.alloc(4096)
    wtile = [wtile0, qa]
    rcols = af.alloc(32)
    wt_b2 = [ab.alloc(4096), ab.alloc(4096)]
    trbuf = ab.alloc(1024)

    wq_steps = []
    wq_last = [e_mcols]
    bc_users = {0: [e_mcols], 1: [e_mcols]}
    pend_wq_tr = []
    pb7_free = [e_mcols]
    trb_free = [e_mcols]
    wb_free = {0: [e_mcols], 1: [e_mcols]}
    wq_tr_done = [e_mcols]
    wq_sp_last = [e_mcols]

    def make_wquant(wdram, ntiles, ncols, nblk, wT_scr, m_col, thr_col):
        st_ = {"prev": [None], "prev_l": [None, None], "prev_tr": [None],
               "red": [], "sp": [None]}

        def mean_slice(t0, t1, start_evs):
            def go():
                for ti in range(t0, t1):
                    buf = wtile[ti % 2]
                    e_l = P.dma("sync", buf[:, 0:ncols],
                                wdram[128 * ti:128 * ti + 128, :],
                                after=(wq_last[:] if ti < t0 + 2 else []) +
                                ([st_["red"][-2]] if len(st_["red"]) >= 2 else []),
                                lane=f"wq_{wT_scr.tensor.name}{ti % 2}")
                    e_r = E("vector", [e_l], "reduce_sum",
                            rcols[:, ti % 32:ti % 32 + 1],
                            buf[:, 0:ncols], axis=AX.X, apply_absolute_value=True)
                    st_["red"].append(e_r)
                wq_last[:] = [st_["red"][-1]]
            return go

        def chain():
            def go():
                red_evs = st_["red"]
                e_acc = E("vector", red_evs, "reduce_sum", cols[:, 28:29],
                          rcols[:, 0:ntiles], axis=AX.X)
                e_ps = E("tensor", [e_acc] + bc_users[0] + bc_users[1],
                         "matmul", p[6][0:1, 0:1],
                         cols[:, 28:29], onecol_c[:, 0:1], start=True, stop=True)
                e_mn = E("vector", [e_ps], "tensor_scalar_mul", cols[0:1, 29:30],
                         p[6][0:1, 0:1], 1.0 / (ntiles * 128 * ncols))
                e_cl = E("gpsimd", [e_mn], "tensor_scalar_max", cols[0:1, 30:31],
                         cols[0:1, 29:30], 1e-5)
                e_bc = E("tensor", [e_cl], "matmul", p[6][:, 4:5],
                         onesc_sb[0:1, 0:128], cols[0:1, 30:31],
                         start=True, stop=True)
                e_m = E("scalar", [e_bc], "activation", out=m_col[:, 0:1],
                        in_=p[6][:, 4:5], func=ACT.Copy)
                e_thr = E("scalar", [e_bc], "activation", out=thr_col[:, 0:1],
                          in_=p[6][:, 4:5], func=ACT.Copy, scale=0.5)
                bc_users[0] = [e_mn, e_m, e_thr]
                bc_users[1] = [e_mn, e_m, e_thr]
                st_["prev_l"] = [list(wq_last) + [e_m, e_thr]]
                st_["prev_tr"] = [list(wq_last) + [e_m, e_thr]]
                wq_last[:] = wq_last[:] + [e_thr]
            return go

        def make_tr_group(ti, wb, e_t, g, is_last):
            def go_tr():
                tevs = []
                for j in range(8):
                    ic = 8 * g + j
                    tevs.append(E("tensor", [e_t] + (pb7_free if j == 0 else []),
                                  "transpose",
                                  pb7[:, 128 * j:128 * j + 128],
                                  wb[:, 128 * ic:128 * ic + 128],
                                  identb[:, 0:128]))
                e_ev = E("scalar", [tevs[-1]] + trb_free, "activation",
                         out=trbuf[:, 0:1024], in_=pb7[:, 0:1024],
                         func=ACT.Copy)
                pb7_free[:] = [e_ev]
                sp = None
                for j in range(8):
                    ic = 8 * g + j
                    sp = P.dma(
                        "gpsimd",
                        wT_scr[128 * ic:128 * ic + 128,
                               128 * ti:128 * ti + 128],
                        trbuf[:, 128 * j:128 * j + 128],
                        after=[e_ev] + (wq_sp_last[:] if j == 0 else []),
                        lane="wq_sp")
                wq_sp_last[:] = [sp]
                trb_free[:] = [sp]
                if is_last:
                    wb_free[ti % 2] = [tevs[-1]]
                wq_tr_done[:] = [tevs[-1], sp]
            go_tr.par = ti % 2
            return go_tr

        def quant_slice(t0, t1):
            def go():
                for ti in range(t0, t1):
                    buf = wtile[0]
                    wb = wt_b2[ti % 2]
                    e_l = P.dma("sync", buf[:, 0:ncols],
                                wdram[128 * ti:128 * ti + 128, :],
                                after=st_["prev_l"][0] + (wq_last[:] if ti == t0 else []),
                                lane=f"wq_{wT_scr.tensor.name}0")
                    e_a = E("vector", [e_l], "tensor_single_scalar",
                            qa[:, 0:ncols], buf[:, 0:ncols], thr_col[:, 0:1],
                            ALU.is_ge)
                    e_b = E("vector", [e_l, e_a], "tensor_scalar",
                            qb[:, 0:ncols], buf[:, 0:ncols], -1.0,
                            thr_col[:, 0:1], ALU.mult, ALU.is_ge)
                    while any(getattr(c, "par", None) == ti % 2
                              for c in pend_wq_tr):
                        pend_wq_tr.pop(0)()
                    e_t = E("vector", [e_a, e_b] + wb_free[ti % 2],
                            "tensor_tensor",
                            wb[:, 0:ncols], qa[:, 0:ncols], qb[:, 0:ncols],
                            ALU.subtract)
                    st_["prev_l"][0] = [e_t]
                    ng = nblk // 8
                    for g in range(ng):
                        pend_wq_tr.append(
                            make_tr_group(ti, wb, e_t, g, g == ng - 1))
                wq_last[:] = [st_["prev_l"][0][0]]
            return go

        return mean_slice, chain, quant_slice, st_

    wg_m, wg_c, wg_q, wg_st = make_wquant(wg, 32, D, 8, wgT_s, mgate_c, cols[:, 32:33])
    wu_m, wu_c, wu_q, wu_st = make_wquant(wu, 32, D, 8, wuT_s, mup_c, cols[:, 33:34])
    wd_m, wd_c, wd_q, wd_st = make_wquant(wd, 8, INNER, 32, wdT_s, mdown_c, cols[:, 34:35])
    wq_steps = (
        [wg_m(0, 16, [e_mcols]), wg_m(16, 32, [e_mcols]), wg_c()]
        + [wg_q(2 * i, 2 * i + 2) for i in range(16)]
        + [wu_m(0, 16, [e_mcols]), wu_m(16, 32, [e_mcols]), wu_c()]
        + [wu_q(2 * i, 2 * i + 2) for i in range(16)]
        + [wd_m(0, 4, [e_mcols]), wd_m(4, 8, [e_mcols]), wd_c()]
        + [wd_q(i, i + 1) for i in range(8)]
    )

    b_prev_loads = [e_mcols]
    b_prev_spills = list(a2b_done[-1:])
    attn_spills = []
    wq_slices = []
    slot_free = {s: [e_mcols] for s in range(4)}
    dstage_free = {s: [e_mcols] for s in range(4)}
    prod_free = {0: [e_mcols], 1: [e_mcols]}

    def emit_b_pair(pr, start_evs, drive=None, pop_tr=None):
        loads = []
        for sb, dscr in ((qh_sb, qs_hi), (ql_sb, qs_lo),
                         (kh_sb, ks_hi), (kl_sb, ks_lo)):
            T = dscr.shape[1]
            for hd_ in range(2):
                hh = 2 * pr + hd_
                loads.append(P.dma("sync", sb[64 * hd_:64 * hd_ + 32, 0:T],
                                   dscr[32 * hh:32 * hh + 32, :],
                                   after=start_evs))
                loads.append(P.dma("sync", sb[64 * hd_ + 32:64 * hd_ + 64, 0:T],
                                   dscr[512 + 32 * hh:512 + 32 * hh + 32, :],
                                   after=start_evs))
        vv = v_sb.rearrange("p (c w) -> p c w", w=130)
        for side in range(2):
            loads.append(P.dma(
                "sync", vv[:, :, 65 * side:65 * side + 64],
                vp_s[:, 130 * pr + 65 * side:130 * pr + 65 * side + 64].rearrange(
                    "(c p) w -> p c w", p=128), after=start_evs))
        em1 = E("vector", [loads[-1]], "memset", vv[:, :, 64:65], 1.0)
        em2 = E("vector", [loads[-1]], "memset", vv[:, :, 129:130], 1.0)
        ev_ld = loads + [em1, em2]

        exp_done = {}
        av_done = {}
        evac_done = [None, None]
        stage_done = {}
        rcp_done = {}
        pend_n = []
        last_l = {}

        def flush_norm(hd_, q0):
            slot = 2 * hd_ + (q0 % 2)
            ds = slice(512 * slot, 512 * slot + 512)
            rr = slice(64 * hd_, 64 * hd_ + 64)
            e_bcx = E("tensor", [rcp_done[slot]] + bc_users[hd_], "matmul",
                      p[6][rr, 0:512],
                      onesc_sb[0:1, 0:64], dstage[0:1, ds],
                      start=True, stop=True)
            e_nn = E("vector", [e_bcx] + prod_free[hd_], "tensor_tensor",
                     prodb[rr, 0:512], anf_b[slot][0:64, 0:512],
                     p[6][rr, 0:512], ALU.mult)
            bc_users[hd_] = [e_nn]
            au_dep = list(start_evs) if q0 == 0 else []
            e_hh = E("vector", [e_nn] + au_dep, "tensor_copy",
                     au_hi[rr, 512 * q0:512 * q0 + 512],
                     prodb[rr, 0:512])
            e_ll = E("vector", [e_hh], "tensor_tensor",
                     au_lo[rr, 512 * q0:512 * q0 + 512],
                     prodb[rr, 0:512],
                     au_hi[rr, 512 * q0:512 * q0 + 512],
                     ALU.subtract)
            prod_free[hd_] = [e_ll]
            slot_free[slot] = [e_nn]
            dstage_free[slot] = [e_bcx]
            last_l[hd_] = e_ll
        for qt in range(4):
            def emit_scores(chn):
                par = chn % 2
                for hd_ in range(2):
                    bank = p[2 * hd_ + par]
                    pe = exp_done.get((hd_, par))
                    dep = [pe] if pe is not None else list(ev_ld) + [e_mcols]
                    row0 = 64 * hd_
                    tp = (64 * hd_, 0)
                    E("tensor", dep, "matmul", bank[:, 0:512],
                      kh_sb[row0:row0 + 64, 128 * chn:128 * chn + 128],
                      qh_sb[row0:row0 + 64, 512 * qt:512 * qt + 512],
                      start=True, stop=False, tile_position=tp)
                    E("tensor", [], "matmul", bank[:, 0:512],
                      kh_sb[row0:row0 + 64, 128 * chn:128 * chn + 128],
                      ql_sb[row0:row0 + 64, 512 * qt:512 * qt + 512],
                      start=False, stop=False, tile_position=tp)
                    e3 = E("tensor", [], "matmul", bank[:, 0:512],
                          kl_sb[row0:row0 + 64, 128 * chn:128 * chn + 128],
                          qh_sb[row0:row0 + 64, 512 * qt:512 * qt + 512],
                          start=False, stop=True, tile_position=tp)
                    adep = [e3]
                    pa = av_done.get((hd_, par))
                    if pa is not None:
                        adep.append(pa)
                    e_x = E("scalar", adep, "activation",
                            out=aT[2 * hd_ + par][:, 0:512],
                            in_=bank[:, 0:512],
                            func=ACT.Exp, scale=scinv_c[:, 0:1],
                            bias=m_cols[:, 2 * pr + hd_:2 * pr + hd_ + 1])
                    exp_done[(hd_, par)] = e_x

            def emit_av(chn):
                par = chn % 2
                for hd_ in range(2):
                    avdep = [exp_done[(hd_, par)]]
                    if chn == 0 and evac_done[hd_] is not None:
                        avdep.append(evac_done[hd_])
                    e_av = E("tensor", avdep, "matmul",
                             p[4 + hd_][0:65, 0:512],
                             v_sb[:, 130 * chn + 65 * hd_:130 * chn + 65 * hd_ + 65],
                             aT[2 * hd_ + par][:, 0:512],
                             start=(chn == 0), stop=(chn == 31))
                    av_done[(hd_, par)] = e_av

            emit_scores(0)
            for chn in range(1, 32):
                emit_scores(chn)
                emit_av(chn - 1)
                if chn == 6:
                    while pend_n:
                        h0, q0 = pend_n.pop(0)
                        flush_norm(h0, q0)
                if chn == 16 and drive is not None:
                    drive()
                if chn in (8, 14, 20, 26) and pop_tr is not None:
                    pop_tr()
            emit_av(31)
            if drive is not None:
                drive()
            for hd_ in range(2):
                slot = 2 * hd_ + (qt % 2)
                e_u = E("vector", [av_done[(hd_, 0)], av_done[(hd_, 1)]]
                        + slot_free[slot],
                        "tensor_copy", anf_b[slot][0:65, 0:512],
                        p[4 + hd_][0:65, 0:512])
                stage_done[slot] = e_u
                evac_done[hd_] = e_u
                pend_n.append((hd_, qt))
            for hd_ in range(2):
                slot = 2 * hd_ + (qt % 2)
                ds = slice(512 * slot, 512 * slot + 512)
                rcp_done[slot] = E(
                    "vector", [stage_done[slot]] + dstage_free[slot],
                    "reciprocal", dstage[0:1, ds],
                    anf_b[slot][64:65, 0:512])
        while pend_n:
            h0, q0 = pend_n.pop(0)
            flush_norm(h0, q0)
        s1 = P.dma("gpsimd", attnU_hi_s[128 * pr:128 * pr + 128, :],
                   au_hi[:, 0:T_OWN], after=[last_l[0], last_l[1]],
                   lane="b_au")
        s2 = P.dma("gpsimd", attnU_lo_s[128 * pr:128 * pr + 128, :],
                   au_lo[:, 0:T_OWN], after=[last_l[0], last_l[1]],
                   lane="b_au")
        return [s1, s2, evac_done[0], evac_done[1]]

    pair_done = b_prev_loads
    wq_i = [0]

    def wq_drive():
        if wq_i[0] < len(wq_steps):
            wq_steps[wq_i[0]]()
            wq_i[0] += 1

    def wq_pop_tr():
        if pend_wq_tr:
            pend_wq_tr.pop(0)()

    for pr in range(8):
        pair_done = emit_b_pair(pr, pair_done, drive=wq_drive,
                                pop_tr=wq_pop_tr)
        attn_spills += pair_done[:2]
    while wq_i[0] < len(wq_steps) or pend_wq_tr:
        wq_pop_tr()
        wq_drive()
    P.barrier()



    P.barrier()

    af.reset(PERSIST_F)
    ab.reset(PERSIST_B)
    xqT_sb = ab.alloc(16384)
    PERSIST_B2 = ab.mark()
    projs2 = [af.alloc(1024), af.alloc(1024)]
    projb2 = [af.alloc(1024), af.alloc(1024)]
    xrow2 = [af.alloc(1024), af.alloc(1024)]
    x2row2 = [af.alloc(1024), af.alloc(1024)]
    ln2row2 = [af.alloc(1024), af.alloc(1024)]
    spacer = af.alloc(1536)
    rq = [af.alloc(1024) for _ in range(3)]

    wstage3 = af.alloc(4096)
    au_h = ab.alloc(2048)
    au_l = ab.alloc(2048)
    au_h2 = [au_h[:, 0:1024], au_h[:, 1024:2048]]
    au_l2 = [au_l[:, 0:1024], au_l[:, 1024:2048]]
    anh = ab.alloc(1024); anl = ab.alloc(1024)
    wo_hi = ab.alloc(8192); wo_lo = ab.alloc(8192)
    xq_b2 = [ab.alloc(1024), ab.alloc(1024)]

    ev_wo = prep_w_hilo(wo, (0, 4), 128, wstage3, wo_hi, wo_lo,
                        list(wq_tr_done))
    ev_wo = prep_w_hilo(wo, (4, 4), 128, wstage3,
                        wo_hi[:, 4096:8192], wo_lo[:, 4096:8192], [ev_wo])

    cn_done = attn_spills[-2:] + [ev_wo]

    def ln_single(src, after, b=36):
        e_s = E("vector", after, "reduce_sum", stats[:, b:b + 1], src, axis=AX.X)
        e_q = E("scalar", list(after) + _sq_chain[:], "activation",
                out=spacer[:, 0:1024],
                in_=src, func=ACT.Square, accum_out=stats[:, b + 1:b + 2])
        _sq_chain[:] = [e_q]
        e_mu = E("gpsimd", [e_s], "tensor_scalar_mul", stats[:, b + 2:b + 3],
                 stats[:, b:b + 1], 1.0 / 1024)
        e_t3 = E("gpsimd", [e_q], "tensor_scalar", stats[:, b + 3:b + 4],
                 stats[:, b + 1:b + 2], 1.0 / 1024, 1e-5, ALU.mult, ALU.add)
        e_m2 = E("vector", [e_mu], "tensor_tensor", stats[:, b + 4:b + 5],
                 stats[:, b + 2:b + 3], stats[:, b + 2:b + 3], ALU.mult)
        e_v = E("gpsimd", [e_t3, e_m2], "tensor_tensor", stats[:, b + 5:b + 6],
                stats[:, b + 3:b + 4], stats[:, b + 4:b + 5], ALU.subtract)
        e_sd = E("scalar", [e_v], "activation", out=stats[:, b + 6:b + 7],
                 in_=stats[:, b + 5:b + 6], func=ACT.Sqrt)
        e_r = E("vector", [e_sd], "reciprocal", stats[:, b + 7:b + 8],
                stats[:, b + 6:b + 7])
        e_sp = E("vector", _sp_chain[:], "tensor_copy",
                 spacer[:, 1024:1536], src[:, 0:512])
        _sp_chain[:] = [e_sp]
        return e_r, e_sp

    prev_au = list(cn_done)
    prev_an = list(cn_done)
    prev_xq = list(cn_done)
    prev_x2w = list(cn_done)
    prev_projw = list(cn_done)
    prev_ln2w = list(cn_done)
    c_bank_free = {(pp, oo): list(cn_done) for pp in range(2) for oo in range(2)}
    sc_free = {(pp, oo): list(cn_done) for pp in range(2) for oo in range(2)}
    c_done = []
    prev_par = {0: dict(au=list(cn_done), x2w=list(cn_done),
                        projw=list(cn_done), ln2w=list(cn_done),
                        xrow=list(cn_done)),
                1: dict(au=list(cn_done), x2w=list(cn_done),
                        projw=list(cn_done), ln2w=list(cn_done),
                        xrow=list(cn_done))}
    pend_tr = []
    prev_xq_par = {0: prev_xq, 1: prev_xq}

    def emit_xq_tr(tc2, e_qc2, xq_b_2):
        tr_evs = []
        tevs = []
        for fc in range(8):
            tevs.append(E("tensor", [e_qc2] + prev_xq, "transpose",
                          pb7[:, 128 * fc:128 * fc + 128],
                          xq_b_2[:, 128 * fc:128 * fc + 128],
                          identb[:, 0:128]))
        for fc in range(8):
            tr_evs.append(E("vector", [tevs[-1]], "tensor_copy",
                           xqT_sb[:, 2048 * fc + 128 * tc2:2048 * fc + 128 * tc2 + 128],
                           pb7[:, 128 * fc:128 * fc + 128]))
        return tr_evs

    for tc in range(16):
        par = tc % 2
        pv = prev_par[par]
        if len(pend_tr) >= 2:
            tc2, e_qc2, xq_b_2 = pend_tr.pop(0)
            tr_evs = emit_xq_tr(tc2, e_qc2, xq_b_2)
            prev_xq = tr_evs[-1:]
            prev_xq_par[tc2 % 2] = tr_evs[-1:]
            c_done = tr_evs[-1:]
        auh, aul = au_h2[par], au_l2[par]
        projb = projb2[par]
        xrow = xrow2[par]
        x2row = x2row2[par]
        ln2row = ln2row2[par]
        xq_b = xq_b2[par]
        sb = 36 + 16 * par
        cb = 36 if par == 0 else 8
        l1 = P.dma("sync", auh[:, 0:1024].rearrange("p (c t) -> p c t", t=128),
                   attnU_hi_s[:, 128 * tc:128 * tc + 128].rearrange(
                       "(c p) t -> p c t", p=128), after=pv["au"],
                   lane=f"c_au{par}")
        l2 = P.dma("sync", aul[:, 0:1024].rearrange("p (c t) -> p c t", t=128),
                   attnU_lo_s[:, 128 * tc:128 * tc + 128].rearrange(
                       "(c p) t -> p c t", p=128), after=pv["au"],
                   lane=f"c_au{par}")
        lx = P.dma("sync", xrow[:, 0:1024],
                   x_own[128 * tc:128 * tc + 128, :], after=pv["xrow"],
                   lane=f"c_x{par}")
        an_evs = [l1, l2]
        wo_evs = []
        for oc in range(2):
            bank = p[2 + 2 * par + oc] if par == 0 else p[4 + oc]
            evm = mm3_group(
                bank[:, 0:512], auh, aul,
                lambda fc: (128 * fc, 128 * fc + 128),
                wo_hi, wo_lo,
                lambda fc, oc=oc: (1024 * fc + 512 * oc, 1024 * fc + 512 * oc + 512),
                8, an_evs + c_bank_free[(par, oc)])
            e_cp = E("scalar", [evm] + sc_free[(par, oc)], "activation",
                     out=projs2[par][:, 512 * oc:512 * oc + 512],
                     in_=bank[:, 0:512], func=ACT.Copy)
            e_p = E("gpsimd", [e_cp] + pv["projw"], "tensor_tensor",
                    projb[:, 512 * oc:512 * oc + 512],
                    projs2[par][:, 512 * oc:512 * oc + 512],
                    bwo_bc[:, 512 * oc:512 * oc + 512], ALU.add)
            sc_free[(par, oc)] = [e_p]
            c_bank_free[(par, oc)] = [e_cp]
            wo_evs.append(e_p)
        pv["au"] = list(wo_evs)
        e_r1, e_sp1 = ln_single(projb[:, 0:1024], wo_evs, b=sb)
        e_x2 = E("vector", [e_r1, e_sp1, lx] + pv["x2w"], "tensor_scalar",
                 x2row[:, 0:1024], projb[:, 0:1024],
                 stats[:, sb + 2:sb + 3], stats[:, sb + 7:sb + 8],
                 ALU.subtract, ALU.mult)
        pv["projw"] = [e_x2]
        e_x2b = E("vector", [e_x2], "tensor_tensor", x2row[:, 0:1024],
                  x2row[:, 0:1024], xrow[:, 0:1024], ALU.add)
        pv["xrow"] = [e_x2b]
        s_x2 = P.dma("gpsimd", x2_s[128 * tc:128 * tc + 128, :],
                     x2row[:, 0:1024], after=[e_x2b], lane=f"c_x2{par}")
        e_r2, e_sp2 = ln_single(x2row[:, 0:1024], [e_x2b], b=sb)
        e_ln2 = E("vector", [e_r2, e_sp2] + pv["ln2w"], "tensor_scalar",
                  ln2row[:, 0:1024], x2row[:, 0:1024],
                  stats[:, sb + 2:sb + 3], stats[:, sb + 7:sb + 8],
                  ALU.subtract, ALU.mult)
        e_mx = E("vector", [e_ln2], "reduce_max", cols[:, cb:cb + 1],
                 ln2row[:, 0:1024], axis=AX.X, apply_absolute_value=True)
        e_mc = E("gpsimd", [e_mx], "tensor_scalar_max", cols[:, cb + 1:cb + 2],
                 cols[:, cb:cb + 1], 1e-5)
        e_rc = E("vector", [e_mc], "reciprocal", cols[:, cb + 2:cb + 3],
                 cols[:, cb + 1:cb + 2])
        e_sx = E("gpsimd", [e_rc], "tensor_scalar_mul",
                 sx_all[:, tc:tc + 1], cols[:, cb + 2:cb + 3], 127.0)
        e_e1 = E("gpsimd", [e_mc], "tensor_scalar_mul",
                 ev1_all[:, tc:tc + 1], cols[:, cb + 1:cb + 2], 1.0 / 127.0)
        e_sg = E("vector", [e_e1], "tensor_tensor", sg_all[:, tc:tc + 1],
                 ev1_all[:, tc:tc + 1], mgate_c[:, 0:1], ALU.mult)
        e_su = E("vector", [e_e1], "tensor_tensor", su_all[:, tc:tc + 1],
                 ev1_all[:, tc:tc + 1], mup_c[:, 0:1], ALU.mult)
        e_u1 = E("vector", [e_sx, e_ln2], "tensor_scalar", rq[0][:, 0:1024],
                 ln2row[:, 0:1024], sx_all[:, tc:tc + 1], C1, ALU.mult, ALU.add)
        e_u2 = E("vector", [e_u1], "tensor_scalar", rq[1][:, 0:1024],
                 ln2row[:, 0:1024], sx_all[:, tc:tc + 1], C2, ALU.mult, ALU.add)
        e_d2 = E("vector", [e_u2], "tensor_single_scalar", rq[1][:, 0:1024],
                 rq[1][:, 0:1024], C2, ALU.subtract)
        e_tq = E("vector", [e_d2], "scalar_tensor_tensor", rq[0][:, 0:1024],
                 rq[0][:, 0:1024], C1, rq[1][:, 0:1024], ALU.subtract, ALU.subtract)
        e_t2 = E("vector", [ev_zm, e_tq], "tensor_single_scalar", rq[0][:, 0:1024],
                 rq[0][:, 0:1024], zmask_c[:, 0:1], ALU.mult)
        e_qq = E("vector", prev_xq_par[par] + [e_t2], "tensor_tensor",
                 xq_b[:, 0:1024],
                 rq[0][:, 0:1024], rq[1][:, 0:1024], ALU.add)
        pv["x2w"] = [s_x2, e_ln2]
        pv["ln2w"] = [e_u2, e_mx]
        pend_tr.append((tc, e_qq, xq_b))
    while pend_tr:
        tc2, e_qc2, xq_b_2 = pend_tr.pop(0)
        tr_evs = emit_xq_tr(tc2, e_qc2, xq_b_2)
        prev_xq = tr_evs[-1:]
        c_done = [s_x2] + tr_evs[-1:]
    P.barrier()

    af.reset(PERSIST_F)
    ab.reset(PERSIST_B2)
    y_row2 = [af.alloc(4096), af.alloc(4096)]
    gbuf = af.alloc(512)
    ubuf = af.alloc(512)
    rqa = af.alloc(2048)
    rqb = af.alloc(2048)
    rga = af.alloc(2048)
    rgb = af.alloc(2048)

    xqT2 = xqT_sb
    wg_sb = ab.alloc(8192)
    wu_sb = ab.alloc(8192)
    yq_b2 = [ab.alloc(4096), ab.alloc(4096)]
    strip = ab.alloc(4096)

    d1_done = []
    pend_d1 = []
    prev_yq = {0: list(c_done), 1: list(c_done)}
    prev_w = {0: list(c_done), 1: list(c_done)}
    prev_y = {0: list(c_done), 1: list(c_done)}
    d1_rq_chain = list(c_done)
    d1_rg_chain = list(c_done)
    prev_strip = list(c_done)
    prev_gu = list(c_done)
    d_bank_free = {i: list(c_done) for i in range(4)}
    for tcp in range(8):
        tcs = (2 * tcp, 2 * tcp + 1)
        ot_evs = {0: [], 1: []}
        for ot in range(8):
            par = ot % 2
            lg = P.dma("sync", wg_sb[:, 4096 * par:4096 * par + 4096].rearrange(
                "p (c t) -> p c t", t=512),
                wgT_s[:, 512 * ot:512 * ot + 512].rearrange(
                    "(c p) t -> p c t", p=128), after=prev_w[par],
                lane=f"wg{par}")
            lu = P.dma("sync", wu_sb[:, 4096 * par:4096 * par + 4096].rearrange(
                "p (c t) -> p c t", t=512),
                wuT_s[:, 512 * ot:512 * ot + 512].rearrange(
                    "(c p) t -> p c t", p=128), after=prev_w[par],
                lane=f"wu{par}")
            last_mm = []
            for ti, tc in enumerate(tcs):
                evg = None
                evu = None
                for fc in range(8):
                    evg = E("tensor", ([lg] + d_bank_free[par]) if fc == 0 else [],
                            "matmul", p[par][:, 0:512],
                            xqT2[:, 2048 * fc + 128 * tc:2048 * fc + 128 * tc + 128],
                            wg_sb[:, 4096 * par + 512 * fc:4096 * par + 512 * fc + 512],
                            start=(fc == 0), stop=(fc == 7))
                for fc in range(8):
                    evu = E("tensor", ([lu] + d_bank_free[2 + par]) if fc == 0 else [],
                            "matmul", p[2 + par][:, 0:512],
                            xqT2[:, 2048 * fc + 128 * tc:2048 * fc + 128 * tc + 128],
                            wu_sb[:, 4096 * par + 512 * fc:4096 * par + 512 * fc + 512],
                            start=(fc == 0), stop=(fc == 7))
                e_g = E("scalar", [evg] + prev_gu, "activation", out=gbuf[:, 0:512],
                        in_=p[par][:, 0:512], func=ACT.Silu,
                        scale=sg_all[:, tc:tc + 1])
                e_u = E("scalar", [evu], "activation", out=ubuf[:, 0:512],
                        in_=p[2 + par][:, 0:512], func=ACT.Copy,
                        scale=su_all[:, tc:tc + 1])
                e_y = E("vector", [e_g, e_u] + (prev_y[ti] if ot == 0 else []),
                        "tensor_tensor",
                        y_row2[ti][:, 512 * ot:512 * ot + 512],
                        gbuf[:, 0:512], ubuf[:, 0:512], ALU.mult)
                prev_gu = [e_y]
                d_bank_free[par] = [e_g, e_y]
                d_bank_free[2 + par] = [e_u, e_y]
                last_mm += [evg, evu]
                ot_evs[ti].append(e_y)
            prev_w[par] = last_mm
        while pend_d1:
            tcd, q_evs_d, yqd = pend_d1.pop(0)
            st_evs = []
            for g in range(4):
                tevs = []
                for j in range(8):
                    ic = 8 * g + j
                    tevs.append(E("tensor",
                                  q_evs_d + prev_strip + (st_evs[-1:] if g else []),
                                  "transpose",
                                  pb7[:, 128 * j:128 * j + 128],
                                  yqd[:, 128 * ic:128 * ic + 128],
                                  identb[:, 0:128]))
                st_evs.append(E("vector", [tevs[-1]], "tensor_copy",
                                strip[:, 1024 * g:1024 * g + 1024],
                                pb7[:, 0:1024]))
            s_st = P.dma("gpsimd", xq2T_s[128 * tcd:128 * tcd + 128, :],
                         strip[:, 0:4096], after=st_evs, lane="d_st")
            prev_strip = [s_st]
            prev_yq[tcd % 2] = [st_evs[-1]]
            d1_done = [s_st]
        for ti, tc in enumerate(tcs):
            y_row = y_row2[ti]
            yq_b = yq_b2[ti]
            e_mx = E("vector", ot_evs[ti], "reduce_max", cols[:, 40:41],
                     y_row[:, 0:4096], axis=AX.X, apply_absolute_value=True)
            e_mc = E("gpsimd", [e_mx], "tensor_scalar_max", cols[:, 41:42],
                     cols[:, 40:41], 1e-5)
            e_rc = E("vector", [e_mc], "reciprocal", cols[:, 42:43], cols[:, 41:42])
            e_sx = E("gpsimd", [e_rc], "tensor_scalar_mul", cols[:, 43:44],
                     cols[:, 42:43], 127.0)
            e_sd = E("gpsimd", [e_mc], "tensor_scalar_mul", cols[:, 44:45],
                     cols[:, 41:42], 1.0 / 127.0)
            e_sd2 = E("vector", [e_sd], "tensor_tensor", sd_all[:, tc:tc + 1],
                      cols[:, 44:45], mdown_c[:, 0:1], ALU.mult)
            ysl0 = y_row[:, 0:2048]
            eh1 = E("vector", [e_sx] + d1_rq_chain, "tensor_scalar",
                    rqa[:, 0:2048], ysl0, cols[:, 43:44], C1, ALU.mult, ALU.add)
            eh2 = E("vector", [eh1], "tensor_scalar", rqb[:, 0:2048],
                    ysl0, cols[:, 43:44], C2, ALU.mult, ALU.add)
            eh3 = E("vector", [eh2], "tensor_single_scalar", rqb[:, 0:2048],
                    rqb[:, 0:2048], C2, ALU.subtract)
            eh4 = E("vector", [eh3], "scalar_tensor_tensor", rqa[:, 0:2048],
                    rqa[:, 0:2048], C1, rqb[:, 0:2048], ALU.subtract,
                    ALU.subtract)
            eh5 = E("vector", [ev_zm, eh4], "tensor_single_scalar",
                    rqa[:, 0:2048], rqa[:, 0:2048], zmask_c[:, 0:1], ALU.mult)
            eh6 = E("vector", prev_yq[ti] + [eh5], "tensor_tensor",
                    yq_b[:, 0:2048], rqa[:, 0:2048], rqb[:, 0:2048], ALU.add)
            d1_rq_chain[:] = [eh6]
            ysl1 = y_row[:, 2048:4096]
            gh1 = E("gpsimd", [e_sx] + d1_rg_chain, "tensor_scalar",
                    rga[:, 0:2048], ysl1, cols[:, 43:44], C1, ALU.mult, ALU.add)
            gh2 = E("gpsimd", [gh1], "tensor_scalar", rgb[:, 0:2048],
                    ysl1, cols[:, 43:44], C2, ALU.mult, ALU.add)
            gh3 = E("gpsimd", [gh2], "tensor_single_scalar", rgb[:, 0:2048],
                    rgb[:, 0:2048], C2, ALU.subtract)
            gh4a = E("gpsimd", [gh3], "tensor_single_scalar", rga[:, 0:2048],
                     rga[:, 0:2048], C1, ALU.subtract)
            gh4 = E("gpsimd", [gh4a], "tensor_tensor", rga[:, 0:2048],
                    rga[:, 0:2048], rgb[:, 0:2048], ALU.subtract)
            gh5 = E("gpsimd", [ev_zm, gh4], "tensor_single_scalar",
                    rga[:, 0:2048], rga[:, 0:2048], zmask_c[:, 0:1], ALU.mult)
            gh6 = E("gpsimd", prev_yq[ti] + [gh5], "tensor_tensor",
                    yq_b[:, 2048:4096], rga[:, 0:2048], rgb[:, 0:2048], ALU.add)
            d1_rg_chain[:] = [gh6]
            q_evs = [eh6, gh6]
            prev_y[ti] = q_evs
            pend_d1.append((tc, list(q_evs), yq_b))
    while pend_d1:
        tcd, q_evs_d, yqd = pend_d1.pop(0)
        st_evs = []
        for g in range(4):
            tevs = []
            for j in range(8):
                ic = 8 * g + j
                tevs.append(E("tensor",
                              q_evs_d + prev_strip + (st_evs[-1:] if g else []),
                              "transpose",
                              pb7[:, 128 * j:128 * j + 128],
                              yqd[:, 128 * ic:128 * ic + 128],
                              identb[:, 0:128]))
            st_evs.append(E("vector", [tevs[-1]], "tensor_copy",
                            strip[:, 1024 * g:1024 * g + 1024], pb7[:, 0:1024]))
        s_st = P.dma("gpsimd", xq2T_s[128 * tcd:128 * tcd + 128, :],
                     strip[:, 0:4096], after=st_evs, lane="d_st")
        prev_strip = [s_st]
        d1_done = [s_st]
    P.barrier()

    af.reset(PERSIST_F)
    ab.reset(PERSIST_B)
    dnrow2 = [af.alloc(1024), af.alloc(1024)]
    x2b2 = [af.alloc(1024), af.alloc(1024)]
    outrow2 = [af.alloc(1024), af.alloc(1024)]
    spacer3 = af.alloc(1536)

    wd_sb = ab.alloc(32768)
    strip22 = [ab.alloc(4096), ab.alloc(4096)]

    e_wd = P.dma("sync", wd_sb[:, 0:32768].rearrange("p (c t) -> p c t", t=1024),
                 wdT_s[:, :].rearrange("(c p) t -> p c t", p=128), after=d1_done)

    spacer = spacer3
    d2p = {pp: dict(str2=[e_wd], out=[e_wd], dnr=[e_wd], so=[e_wd])
           for pp in range(2)}
    d2_bank = {(pp, oo): [e_wd] for pp in range(2) for oo in range(2)}
    out_dmas = []
    for tc in range(16):
        par = tc % 2
        pv2 = d2p[par]
        strip2 = strip22[par]
        dnrow = dnrow2[par]
        x2b = x2b2[par]
        outrow = outrow2[par]
        sb2 = 36 + 16 * par
        l_st = P.dma("sync", strip2[:, 0:4096],
                     xq2T_s[128 * tc:128 * tc + 128, :], after=pv2["str2"],
                     lane=f"d2s{par}")
        lx2 = P.dma("sync", x2b[:, 0:1024],
                    x2_s[128 * tc:128 * tc + 128, :], after=pv2["out"],
                    lane=f"d2x{par}")
        dn_evs = []
        for oc in range(2):
            bank = p[2 + oc] if par == 0 else p[4 + oc]
            evm = None
            for ic in range(32):
                evm = E("tensor",
                        ([l_st, e_wd] + d2_bank[(par, oc)]) if ic == 0 else [],
                        "matmul", bank[:, 0:512],
                        strip2[:, 128 * ic:128 * ic + 128],
                        wd_sb[:, 1024 * ic + 512 * oc:1024 * ic + 512 * oc + 512],
                        start=(ic == 0), stop=(ic == 31))
            e_dn = E("scalar", [evm], "activation",
                     out=dnrow[:, 512 * oc:512 * oc + 512],
                     in_=bank[:, 0:512], func=ACT.Copy,
                     scale=sd_all[:, tc:tc + 1])
            d2_bank[(par, oc)] = [e_dn] + pv2["dnr"]
            dn_evs.append(e_dn)
        e_r3, e_sp3 = ln_single(dnrow[:, 0:1024], dn_evs, b=sb2)
        e_o = E("vector", [e_r3, e_sp3, lx2] + pv2["so"], "tensor_scalar",
                outrow[:, 0:1024], dnrow[:, 0:1024],
                stats[:, sb2 + 2:sb2 + 3], stats[:, sb2 + 7:sb2 + 8],
                ALU.subtract, ALU.mult)
        e_o2 = E("vector", [e_o], "tensor_tensor", outrow[:, 0:1024],
                 outrow[:, 0:1024], x2b[:, 0:1024], ALU.add)
        s_o = P.dma("gpsimd", out_d[128 * tc:128 * tc + 128, :],
                    outrow[:, 0:1024], after=[e_o2], lane=f"out{par}")
        out_dmas.append(s_o)
        pv2["str2"] = [dn_evs[-1]]
        pv2["out"] = [e_o2]
        pv2["dnr"] = [e_o]
        pv2["so"] = [s_o]
    P.emit("gpsimd", lambda g: g.nop(), after=out_dmas)

    P.finalize()
    return nc



def _host_tables():
    pos = np.arange(S, dtype=np.float64)[:, None]
    ids = np.arange(D // 2, dtype=np.float64)
    theta = 10000.0 ** (-2.0 * ids) / (D // 2)
    r = pos * theta[None, :]
    cosT = np.cos(r).T
    sinT = np.sin(r).T
    cos32 = np.ascontiguousarray(cosT.astype(np.float32))
    sin32 = np.ascontiguousarray(sinT.astype(np.float32))
    lo_c = (cosT[0] - cos32[0].astype(np.float64)).astype(np.float32)[None, :]
    lo_s = (sinT[0] - sin32[0].astype(np.float32).astype(np.float64)
            ).astype(np.float32)[None, :]
    lo_s = (sinT[0] - sin32[0].astype(np.float64)).astype(np.float32)[None, :]
    return cos32, sin32, np.ascontiguousarray(lo_c), np.ascontiguousarray(lo_s)


def _bias_cols(b, nch):
    return np.ascontiguousarray(np.asarray(b, np.float32).reshape(nch, 128).T)


_NC_CACHE = [None]


def _prep(inp):
    f32 = np.float32
    X = np.ascontiguousarray(np.asarray(inp["L0_attn_x"], f32))
    perm = np.concatenate([np.arange(0, D, 2), np.arange(1, D, 2)])

    w = {k: np.ascontiguousarray(np.asarray(v, f32)) for k, v in inp.items()
         if k != "L0_attn_x"}
    wuq_p = np.ascontiguousarray(w["L0_attn_Wu_q_w"][:, perm])
    wqr_p = np.ascontiguousarray(w["L0_attn_Wq_r_w"][:, perm])
    wuk_p = np.ascontiguousarray(w["L0_attn_Wu_k_w"][:, perm])
    wkr_p = np.ascontiguousarray(w["L0_attn_Wk_r_w"][:, perm])
    buq_p = w["L0_attn_Wu_q_b"][perm]
    bqr_p = w["L0_attn_Wq_r_b"][perm]
    buk_p = w["L0_attn_Wu_k_b"][perm]
    bkr_p = w["L0_attn_Wk_r_b"][perm]

    cos32, sin32, lo_c, lo_s = _host_tables()

    ind = np.zeros((D, NH), f32)
    for r_ in range(D):
        hh = (r_ // 32) if r_ < 512 else ((r_ - 512) // 32)
        ind[r_, hh] = 1.0
    sel = np.zeros((NH, 1024), f32)
    for j in range(1024):
        sel[2 * (j // 128) + (1 if (j % 128) >= 64 else 0), j] = 1.0

    common = {
        "wdq": w["L0_attn_Wd_q_w"], "wdkv": w["L0_attn_Wd_kv_w"],
        "wuq": wuq_p, "wqr": wqr_p, "wuk": wuk_p, "wkr": wkr_p,
        "wuv": w["L0_attn_Wu_v_w"], "wo": w["L0_attn_Wo_w"],
        "wg": w["L0_ffn_w_gate"], "wu": w["L0_ffn_w_up"],
        "wd": w["L0_ffn_w_down"],
        "bdq_c": _bias_cols(w["L0_attn_Wd_q_b"], 4),
        "bdkv_c": _bias_cols(w["L0_attn_Wd_kv_b"], 4),
        "buq_c": _bias_cols(buq_p, 8),
        "bqr_c": _bias_cols(bqr_p, 8),
        "buk_c": _bias_cols(buk_p, 8),
        "bkr_c": _bias_cols(bkr_p, 8),
        "bvb_r": np.ascontiguousarray(w["L0_attn_Wu_v_b"][None, :]),
        "bwo_r": np.ascontiguousarray(w["L0_attn_Wo_b"][None, :]),
        "identf": np.eye(128, dtype=f32),
        "identb": np.eye(128, dtype=ml_dtypes.bfloat16),
        "ind_norm": ind, "sel_pairs": sel,
        "id16": np.eye(NH, dtype=f32),
        "ones16": np.ones((NH, 128), f32),
        "onescol": np.ones((1, 128), f32),
        "cos_k": cos32, "sin_k": sin32,
        "lo_ck": lo_c, "lo_sk": lo_s,
    }

    return common, X, cos32, sin32, lo_c, lo_s


def _in_maps(inp):
    common, X, cos32, sin32, lo_c, lo_s = _prep(inp)
    in_maps = []
    for c in range(8):
        b, hf = c // 2, c % 2
        m = dict(common)
        m["x_all"] = np.ascontiguousarray(X[b])
        m["x_own"] = np.ascontiguousarray(X[b, 2048 * hf:2048 * hf + 2048])
        m["cos_q"] = np.ascontiguousarray(cos32[:, 2048 * hf:2048 * hf + 2048])
        m["sin_q"] = np.ascontiguousarray(sin32[:, 2048 * hf:2048 * hf + 2048])
        m["lo_cq"] = np.ascontiguousarray(lo_c[:, 2048 * hf:2048 * hf + 2048])
        m["lo_sq"] = np.ascontiguousarray(lo_s[:, 2048 * hf:2048 * hf + 2048])
        in_maps.append(m)
    return in_maps


def kernel(**inp):
    in_maps = _in_maps(inp)
    if _NC_CACHE[0] is None:
        _NC_CACHE[0] = build_nc()
    nc = _NC_CACHE[0]
    res = run_bass_kernel_spmd(nc, in_maps, core_ids=list(range(8)))

    out = np.empty((4, S, D), np.float32)
    for c in range(8):
        b, hf = c // 2, c % 2
        out[b, 2048 * hf:2048 * hf + 2048] = np.asarray(res.results[c]["out"],
                                                        np.float32)
    return out


# version 23
# speedup vs baseline: 1.1411x; 1.1411x over previous
import numpy as np
import ml_dtypes
import concourse.bass as bass
import concourse.mybir as mybir
from concourse.bass_utils import run_bass_kernel_spmd

F32 = mybir.dt.float32
BF16 = mybir.dt.bfloat16
AX = mybir.AxisListType
ALU = mybir.AluOpType
ACT = mybir.ActivationFunctionType

C1 = float(np.float32(1.5 * 2.0 ** 23))
C2 = float(np.float32(1.5 * 2.0 ** 52))
SCALE_INV = float(np.float32(1.0 / np.sqrt(np.float32(1040.0))))
NORM_SLACK = float(np.float32(1.004 * 1.004))

D = 1024
S = 4096
T_OWN = 2048
NH = 16
HD = 64
INNER = 4096

ENGINES = ("tensor", "vector", "scalar", "sync", "gpsimd")


class Ev:
    __slots__ = ("eng", "dma", "needed", "sem", "val", "seq", "lane")

    def __init__(self, eng, dma):
        self.eng, self.dma, self.needed = eng, dma, False
        self.sem, self.val = None, None
        self.seq = 0
        self.lane = None


class Prog:

    def __init__(self, nc):
        self.nc = nc
        self.streams = {e: [] for e in ENGINES}
        self.seq = 0
        self.last_dma = {}

    def emit(self, eng, fn, after=(), dma=False, lane=None):
        ev = Ev(eng, dma)
        ev.seq = self.seq
        ev.lane = lane
        self.seq += 1
        deps = [a for a in after if a is not None]
        self.streams[eng].append((fn, deps, ev))
        return ev

    def dma(self, eng, out, in_, after=(), lane=None, **kw):
        ev = self.emit(eng, lambda e: e.dma_start(out=out, in_=in_, **kw),
                       after=after, dma=True, lane=lane)
        self.last_dma[(eng, lane)] = ev
        return ev

    def barrier(self):
        evs = [self.emit(e, _nopf(e),
                         after=[ev for (ee, ln), ev in self.last_dma.items()
                                if ee == e])
               for e in ENGINES]
        outs = []
        for e in ENGINES:
            deps = [ev for ev in evs if ev.eng != e]
            outs.append(self.emit(e, _nopf(e), after=deps))
        return outs

    def finalize(self):
        nc = self.nc
        for e in ENGINES:
            for fn, deps, ev in self.streams[e]:
                for a in deps:
                    a.needed = True
        sems = {}
        for e in ENGINES:
            sems[(e, False)] = nc.alloc_semaphore("c_" + e)
            sems[(e, True)] = nc.alloc_semaphore("d_" + e)
        lane_sems = {}
        for e in ENGINES:
            for fn, deps, ev in self.streams[e]:
                if ev.dma and ev.lane is not None and (e, ev.lane) not in lane_sems:
                    lane_sems[(e, ev.lane)] = nc.alloc_semaphore(
                        f"dl_{e}_{ev.lane}")
        issued_before = {}
        dma_seqs = {e: [] for e in ENGINES}
        lane_cnt = {}
        for e in ENGINES:
            cnt = {False: 0, True: 0}
            for fn, deps, ev in self.streams[e]:
                issued_before[id(ev)] = cnt[True]
                if ev.dma:
                    ev.needed = True
                if ev.needed and ev.dma and ev.lane is not None:
                    key = (e, ev.lane)
                    lane_cnt[key] = lane_cnt.get(key, 0) + 16
                    ev.sem = lane_sems[key]
                    ev.val = lane_cnt[key]
                elif ev.needed:
                    cnt[ev.dma] += 16 if ev.dma else 1
                    ev.sem = sems[(e, ev.dma)]
                    ev.val = cnt[ev.dma]
                    if ev.dma:
                        dma_seqs[e].append((ev.seq, ev.val))

        import bisect
        def q_total_at(qeng, seq):
            lst = dma_seqs[qeng]
            i = bisect.bisect_left(lst, (seq, -1))
            return lst[i - 1][1] if i else 0

        def run(eng_obj, e):
            for fn, deps, ev in self.streams[e]:
                waits = {}
                if ev.needed and not ev.dma and ev.val > 1:
                    waits[id(ev.sem)] = (ev.sem, ev.val - 1)
                for a in deps:
                    if a.eng == e and a.dma and a.lane is None:
                        sem, val = sems[(e, True)], issued_before[id(ev)]
                        if val == 0:
                            continue
                    elif a.dma and a.lane is None:
                        sem, val = a.sem, max(a.val, q_total_at(a.eng, ev.seq))
                    else:
                        sem, val = a.sem, a.val
                    key = id(sem)
                    if key not in waits or waits[key][1] < val:
                        waits[key] = (sem, val)
                for sem, val in waits.values():
                    eng_obj.wait_ge(sem, val)
                ins = fn(eng_obj)
                if ev.needed and ins is not None:
                    ins.then_inc(ev.sem, 16 if ev.dma else 1)

        with nc.Block() as block:
            @block.tensor
            def _(t):
                run(t, "tensor")

            @block.vector
            def _(v):
                run(v, "vector")

            @block.scalar
            def _(s):
                run(s, "scalar")

            @block.sync
            def _(y):
                run(y, "sync")

            @block.gpsimd
            def _(g):
                run(g, "gpsimd")


def _nopf(e):
    return lambda eng: eng.nop()


class Arena:

    def __init__(self, ap, total):
        self.ap, self.total, self.off = ap, total, 0

    def alloc(self, n):
        assert self.off + n <= self.total, f"arena overflow {self.off}+{n}>{self.total}"
        sl = self.ap[:, self.off:self.off + n]
        self.off += n
        return sl

    def mark(self):
        return self.off

    def reset(self, mark):
        self.off = mark



NF = 28500
NB = 45500


def build_nc():
    nc = bass.Bass("TRN2", target_bir_lowering=False, debug=False)

    def din(name, shape, dt=F32):
        return nc.dram_tensor(name, list(shape), dt, kind="ExternalInput").ap()

    x_all = din("x_all", (S, D))
    x_own = din("x_own", (T_OWN, D))
    wdq = din("wdq", (D, 512)); wdkv = din("wdkv", (D, 512))
    wuq = din("wuq", (512, D)); wqr = din("wqr", (512, D))
    wuk = din("wuk", (512, D)); wkr = din("wkr", (D, D))
    wuv = din("wuv", (512, D)); wo = din("wo", (D, D))
    wg = din("wg", (INNER, D)); wu = din("wu", (INNER, D)); wd = din("wd", (D, INNER))
    bdq_c = din("bdq_c", (128, 4)); bdkv_c = din("bdkv_c", (128, 4))
    buq_c = din("buq_c", (128, 8)); bqr_c = din("bqr_c", (128, 8))
    buk_c = din("buk_c", (128, 8)); bkr_c = din("bkr_c", (128, 8))
    bvb_r = din("bvb_r", (1, D)); bwo_r = din("bwo_r", (1, D))
    identf_d = din("identf", (128, 128))
    identb_d = din("identb", (128, 128), BF16)
    ind_d = din("ind_norm", (D, NH))
    sel_d = din("sel_pairs", (NH, 1024))
    id16_d = din("id16", (NH, NH))
    ones16_d = din("ones16", (NH, 128))
    onesc_d = din("onescol", (1, 128))
    cosq_d = din("cos_q", (512, T_OWN)); sinq_d = din("sin_q", (512, T_OWN))
    cosk_d = din("cos_k", (512, S)); sink_d = din("sin_k", (512, S))
    locq_d = din("lo_cq", (1, T_OWN)); losq_d = din("lo_sq", (1, T_OWN))
    lock_d = din("lo_ck", (1, S)); losk_d = din("lo_sk", (1, S))

    out_d = nc.dram_tensor("out", [T_OWN, D], F32, kind="ExternalOutput").ap()

    def scr(name, shape, dt):
        return nc.dram_tensor(name, list(shape), dt, kind="Internal").ap()

    lnTk_hi = scr("lnTk_hi", (D, S), BF16); lnTk_lo = scr("lnTk_lo", (D, S), BF16)
    ctq_hi = scr("ctq_hi", (512, T_OWN), BF16); ctq_lo = scr("ctq_lo", (512, T_OWN), BF16)
    ctkv_hi = scr("ctkv_hi", (512, S), BF16); ctkv_lo = scr("ctkv_lo", (512, S), BF16)
    qs_hi = scr("qs_hi", (D, T_OWN), BF16); qs_lo = scr("qs_lo", (D, T_OWN), BF16)
    ks_hi = scr("ks_hi", (D, S), BF16); ks_lo = scr("ks_lo", (D, S), BF16)
    vp_s = scr("vp", (S, 1040), F32)
    wgq_s = scr("wgq", (INNER, D), BF16)
    wuq_s2 = scr("wuq_q", (INNER, D), BF16)
    wdq_s2 = scr("wdq_q", (D, INNER), BF16)
    wgT_s = scr("wgT", (D, INNER), BF16)
    wuT_s = scr("wuT", (D, INNER), BF16)
    wdT_s = scr("wdT", (INNER, D), BF16)
    x2_s = scr("x2", (T_OWN, D), F32)
    xq2T_s = scr("xq2T", (T_OWN, INNER), BF16)
    attnU_hi_s = scr("attnU_hi", (D, T_OWN), BF16)
    attnU_lo_s = scr("attnU_lo", (D, T_OWN), BF16)

    PF = nc.alloc_sbuf_tensor("poolF", [128, NF], F32).ap()
    PB = nc.alloc_sbuf_tensor("poolB", [128, NB], BF16).ap()
    p = [nc.alloc_psum_tensor(f"p{i}", [128, 512], F32).ap() for i in range(7)]
    pb7 = nc.alloc_psum_tensor("pb7", [128, 1024], BF16).ap()

    af = Arena(PF, NF)
    ab = Arena(PB, NB)

    identf = af.alloc(128)
    stats = af.alloc(64)
    cols = af.alloc(64)
    nrm_q = af.alloc(512)
    nrm_k = af.alloc(512)
    m_cols = af.alloc(16)
    ind_sb = af.alloc(128)
    sel_sb = af.alloc(1024)
    id16_sb = af.alloc(16)
    ones16_sb = af.alloc(128)
    onesc_sb = af.alloc(128)
    bvb_bc = af.alloc(1024)
    bwo_bc = af.alloc(1024)
    biasc = af.alloc(40)
    mgate_c = af.alloc(1)
    mup_c = af.alloc(1)
    mdown_c = af.alloc(1)
    zmask_c = af.alloc(1)
    onecol_c = af.alloc(1)
    scinv_c = af.alloc(1)
    sx_all = af.alloc(16)
    ev1_all = af.alloc(16)
    sg_all = af.alloc(16)
    su_all = af.alloc(16)
    sd_all = af.alloc(16)
    PERSIST_F = af.mark()

    identb = ab.alloc(128)
    PERSIST_B = ab.mark()

    P = Prog(nc)

    ev_c = []
    ev_c.append(P.dma("sync", identf[:, :], identf_d[:, :]))
    ev_c.append(P.dma("sync", identb[:, :], identb_d[:, :]))
    ev_c.append(P.dma("sync", ind_sb.rearrange("p (c h) -> p c h", h=NH),
                     ind_d[:, :].rearrange("(c p) h -> p c h", p=128)))
    ev_c.append(P.dma("sync", sel_sb[0:NH, :], sel_d[:, :]))
    ev_c.append(P.dma("sync", id16_sb[0:NH, :], id16_d[:, :]))
    ev_c.append(P.dma("sync", ones16_sb[0:NH, :], ones16_d[:, :]))
    ev_c.append(P.dma("sync", onesc_sb[0:1, :], onesc_d[:, :]))
    bias_dmas = []
    for dst_off, src in ((0, bdq_c), (4, bdkv_c), (8, buq_c), (16, bqr_c),
                         (24, buk_c), (32, bkr_c)):
        n = src.shape[1]
        bias_dmas.append(P.dma("sync", biasc[:, dst_off:dst_off + n], src[:, :]))
    brow = PF[:, PERSIST_F:PERSIST_F + 2048]
    ev_b1 = P.dma("sync", brow[0:1, 0:D], bvb_r[:, :])
    ev_b2 = P.dma("sync", brow[0:1, 1024:1024 + D], bwo_r[:, :])
    bc_ev = []
    for j, (src_off, dst) in enumerate(((0, bvb_bc), (1024, bwo_bc))):
        for half in range(2):
            e1 = P.emit("tensor", (lambda so=src_off, hf=half: lambda t: t.matmul(
                p[6][:, 0:512], onesc_sb[0:1, 0:128],
                brow[0:1, so + 512 * hf:so + 512 * hf + 512],
                start=True, stop=True))(),
                after=[ev_b1, ev_b2, ev_c[6]] + bc_ev[-1:])
            e2 = P.emit("vector", (lambda d=dst, hf=half: lambda v: v.tensor_copy(
                d[:, 512 * hf:512 * hf + 512], p[6][:, 0:512]))(), after=[e1])
            bc_ev.append(e2)
    ev_one = P.emit("vector", lambda v: v.memset(onecol_c[:, 0:1], 1.0))
    ev_za = P.emit("gpsimd", lambda g: g.tensor_scalar(
        cols[:, 0:1], onecol_c[:, 0:1], C1, C1, ALU.add, ALU.subtract), after=[ev_one])
    ev_zb = P.emit("gpsimd", lambda g: g.tensor_scalar(
        cols[:, 1:2], onecol_c[:, 0:1], C2, C2, ALU.add, ALU.subtract), after=[ev_one])
    ev_zm = P.emit("vector", lambda v: v.tensor_tensor(
        zmask_c[:, 0:1], cols[:, 0:1], cols[:, 1:2], ALU.subtract), after=[ev_za, ev_zb])
    ev_s0 = P.emit("vector", lambda v: v.memset(cols[:, 45:46], 1040.0))
    ev_s1 = P.emit("scalar", lambda s: s.activation(
        out=cols[:, 46:47], in_=cols[:, 45:46], func=ACT.Sqrt), after=[ev_s0])
    ev_s2 = P.emit("vector", lambda v: v.reciprocal(cols[:, 47:48], cols[:, 46:47]),
                   after=[ev_s1])
    ev_s3 = P.emit("gpsimd", lambda g: g.memset(cols[:, 48:49], SCALE_INV))
    ev_s4 = P.emit("gpsimd", lambda g: g.tensor_tensor(
        cols[:, 49:50], cols[:, 48:49], cols[:, 47:48], ALU.subtract),
        after=[ev_s2, ev_s3])
    ev_s5 = P.emit("vector", lambda v: v.tensor_single_scalar(
        cols[:, 50:51], cols[:, 49:50], zmask_c[:, 0:1], ALU.mult),
        after=[ev_s4, ev_zm])
    ev_sc = P.emit("gpsimd", lambda g: g.tensor_tensor(
        scinv_c[:, 0:1], cols[:, 47:48], cols[:, 50:51], ALU.add),
        after=[ev_s5])
    init_evs = ev_c + bias_dmas + bc_ev + [ev_zm, ev_sc]

    _sq_chain = []
    _sp_chain = []

    def E(eng, after, op, *args, **kw):
        return P.emit(eng, lambda e, _o=op, _a=args, _k=kw: getattr(e, _o)(*_a, **_k),
                      after=after)

    def prep_w_hilo(wdram, csel, cpart, stage, dst_hi, dst_lo, after):
        row0, nch = csel
        ocols = wdram.shape[1]
        n = nch * ocols
        src = wdram[row0 * 128:(row0 + nch) * 128, :].rearrange(
            "(c p) o -> p c o", p=128)
        ev_l = P.dma("sync",
                     stage[:, 0:n].rearrange("p (c o) -> p c o", o=ocols),
                     src, after=after)
        ev_h = E("vector", [ev_l], "tensor_copy", dst_hi[:, 0:n], stage[:, 0:n])
        ev_o = E("vector", [ev_h], "tensor_tensor", dst_lo[:, 0:n], stage[:, 0:n],
                 dst_hi[:, 0:n], ALU.subtract)
        return ev_o

    def ln_batch(src, dst, nrc, sqd, after, after_apply=()):
        ev_s, ev_q = [], []
        for rc in range(nrc):
            sl = src[:, 1024 * rc:1024 * rc + 1024]
            ev_s.append(E("vector", after, "reduce_sum",
                          stats[:, rc:rc + 1], sl, axis=AX.X))
            ev_q.append(E("scalar", list(after) + _sq_chain[:], "activation",
                          out=sqd[:, 0:1024], in_=sl, func=ACT.Square,
                          accum_out=stats[:, 4 + rc:5 + rc]))
            _sq_chain[:] = [ev_q[-1]]
        ev_mu, ev_t = [], []
        for rc in range(nrc):
            ev_mu.append(E("vector", [ev_s[rc]], "tensor_scalar_mul",
                           stats[:, 8 + rc:9 + rc], stats[:, rc:rc + 1], 1.0 / 1024))
        for rc in range(nrc):
            ev_t.append(E("vector", [ev_q[rc]], "tensor_scalar",
                          stats[:, 12 + rc:13 + rc], stats[:, 4 + rc:5 + rc],
                          1.0 / 1024, 1e-5, ALU.mult, ALU.add))
        ev_m2 = []
        for rc in range(nrc):
            ev_m2.append(E("vector", [ev_mu[rc]], "tensor_tensor",
                           stats[:, 16 + rc:17 + rc], stats[:, 8 + rc:9 + rc],
                           stats[:, 8 + rc:9 + rc], ALU.mult))
        ev_v = []
        for rc in range(nrc):
            ev_v.append(E("vector", [ev_t[rc], ev_m2[rc]], "tensor_tensor",
                          stats[:, 20 + rc:21 + rc], stats[:, 12 + rc:13 + rc],
                          stats[:, 16 + rc:17 + rc], ALU.subtract))
        ev_sd = []
        for rc in range(nrc):
            ev_sd.append(E("scalar", [ev_v[rc]], "activation",
                           out=stats[:, 24 + rc:25 + rc],
                           in_=stats[:, 20 + rc:21 + rc], func=ACT.Sqrt))
        ev_r = []
        for rc in range(nrc):
            ev_r.append(E("vector", [ev_sd[rc]], "reciprocal",
                          stats[:, 28 + rc:29 + rc], stats[:, 24 + rc:25 + rc]))
        ev_a = []
        for rc in range(nrc):
            ev_a.append(E("vector", list(after_apply) + [ev_r[rc]], "tensor_scalar",
                          dst[:, 1024 * rc:1024 * rc + 1024],
                          src[:, 1024 * rc:1024 * rc + 1024],
                          stats[:, 8 + rc:9 + rc], stats[:, 28 + rc:29 + rc],
                          ALU.subtract, ALU.mult))
        return ev_a

    def transpose_hilo(src, dst_hi, dst_lo, nrc, ev_src, ev_free):
        evs = []
        prev_evac = {0: list(ev_free), 1: list(ev_free)}
        for fc in range(8):
            bank = p[fc % 2]
            tevs = []
            for rc in range(nrc):
                tevs.append(E("tensor",
                              [ev_src[rc]] + prev_evac[fc % 2],
                              "transpose",
                              bank[:, 128 * rc:128 * rc + 128],
                              src[:, 1024 * rc + 128 * fc:1024 * rc + 128 * fc + 128],
                              identf[:, 0:128]))
            dh = dst_hi[:, 512 * fc:512 * fc + 128 * nrc]
            dl = dst_lo[:, 512 * fc:512 * fc + 128 * nrc]
            pv = bank[:, 0:128 * nrc]
            e_h = E("scalar", [tevs[-1]], "activation",
                    out=dh, in_=pv, func=ACT.Copy)
            e_l = E("vector", [e_h], "tensor_tensor", dl, pv, dh, ALU.subtract)
            prev_evac[fc % 2] = [e_l]
            evs.append(e_l)
        return evs

    def mm3_group(bank, lhs_hi, lhs_lo, lhs_sl, rhs_hi, rhs_lo, rhs_sl, nfc,
                  after, tile_position=None, after_fc=None):
        ev = None
        first = True
        for fc in range(nfc):
            wsl = lhs_sl(fc)
            xsl = rhs_sl(fc)
            dep = list(after) if first else []
            if after_fc is not None:
                dep = dep + after_fc(fc)
            for pi, (wop, xop) in enumerate(((lhs_hi, rhs_hi), (lhs_hi, rhs_lo),
                                             (lhs_lo, rhs_hi))):
                last = (fc == nfc - 1 and pi == 2)
                ev = E("tensor", dep, "matmul",
                       bank, wop[:, wsl[0]:wsl[1]], xop[:, xsl[0]:xsl[1]],
                       start=first, stop=last,
                       **({"tile_position": tile_position} if tile_position else {}))
                dep = []
                first = False
        return ev

    af.reset(PERSIST_F)
    ab.reset(PERSIST_B)
    af.alloc(2048)
    wstage = af.alloc(4096)
    x_sb = af.alloc(4096)
    ln_sb = af.alloc(4096)
    sqd = af.alloc(1024)
    ct_f32 = af.alloc(2048)
    v_f32 = wstage

    wdq_hi = ab.alloc(4096); wdq_lo = ab.alloc(4096)
    wdkv_hi = ab.alloc(4096); wdkv_lo = ab.alloc(4096)
    wuv_hi = ab.alloc(4096); wuv_lo = ab.alloc(4096)
    lnT_hi = ab.alloc(4096); lnT_lo = ab.alloc(4096)
    ct_hi = ab.alloc(2048); ct_lo = ab.alloc(2048)
    ctk_hi = ab.alloc(2048); ctk_lo = ab.alloc(2048)

    ev = prep_w_hilo(wdq, (0, 8), 128, wstage, wdq_hi, wdq_lo, init_evs)
    ev = prep_w_hilo(wdkv, (0, 8), 128, wstage, wdkv_hi, wdkv_lo, [ev])
    ev_wprep = prep_w_hilo(wuv, (0, 4), 128, wstage, wuv_hi, wuv_lo, [ev])

    def emit_a1(src_x, ntiles, lnTh_s, lnTl_s, cth_s, ctl_s, w_hi, w_lo,
                bias_o, do_kv, start_evs, spill_lnT=True):
        prev_lnfree = list(start_evs)
        prev_xfree = list(start_evs)
        prev_ctfree = list(start_evs)
        prev_vfree = list(start_evs)
        prev_lnT_readers = list(start_evs)
        bank_free = {i: list(start_evs) for i in range(2, 6)}
        for st in range(ntiles):
            e_x = P.dma("sync",
                        x_sb.rearrange("p (c f) -> p c f", f=1024),
                        src_x[512 * st:512 * st + 512, :].rearrange(
                            "(c p) f -> p c f", p=128),
                        after=prev_xfree)
            ev_app = ln_batch(x_sb, ln_sb, 4, sqd, [e_x], after_apply=prev_lnfree)
            prev_xfree = list(ev_app)
            ev_tr = transpose_hilo(ln_sb, lnT_hi, lnT_lo, 4, ev_app,
                                   prev_lnT_readers)
            prev_lnfree = ev_tr[-8:]
            spill_evs = []
            if spill_lnT:
                e_s1 = P.dma("gpsimd", lnTh_s[:, 512 * st:512 * st + 512].rearrange(
                    "(c p) t -> p c t", p=128),
                    lnT_hi[:, 0:4096].rearrange("p (c t) -> p c t", t=512),
                    after=ev_tr, lane="a_lnt")
                e_s2 = P.dma("gpsimd", lnTl_s[:, 512 * st:512 * st + 512].rearrange(
                    "(c p) t -> p c t", p=128),
                    lnT_lo[:, 0:4096].rearrange("p (c t) -> p c t", t=512),
                    after=ev_tr, lane="a_lnt")
                spill_evs = [e_s1, e_s2]
                lnT_spill_evs = [e_s1, e_s2]
            if not spill_lnT:
                lnT_spill_evs = []
            if do_kv:
                groups = [(wdkv_hi, wdkv_lo, 4 + 0, ctk_hi, ctk_lo, cth_s, ctl_s)]
            else:
                groups = [(w_hi, w_lo, bias_o, ct_hi, ct_lo, cth_s, ctl_s)]
            ct_evs = []
            lnT_reader_evs = []
            for (wh, wl, boff, chid, clod, chs, cls) in groups:
                for oc in range(4):
                    bank = p[2 + oc % 4]
                    evm = mm3_group(
                        bank[:, 0:512], wh, wl,
                        lambda fc, oc=oc: (512 * fc + 128 * oc, 512 * fc + 128 * oc + 128),
                        lnT_hi, lnT_lo,
                        lambda fc: (512 * fc, 512 * fc + 512),
                        8, bank_free[2 + oc % 4],
                        after_fc=lambda fc: [ev_tr[fc]])
                    e_ev = E("scalar", [evm] + prev_ctfree, "activation",
                             out=ct_f32[:, 512 * oc:512 * oc + 512],
                             in_=bank[:, 0:512], func=ACT.Identity,
                             bias=biasc[:, boff + oc:boff + oc + 1])
                    lnT_reader_evs.append(evm)
                    e_h = E("vector", [e_ev] + prev_ctfree, "tensor_copy",
                            chid[:, 512 * oc:512 * oc + 512],
                            ct_f32[:, 512 * oc:512 * oc + 512])
                    e_l = E("vector", [e_h], "tensor_tensor",
                            clod[:, 512 * oc:512 * oc + 512],
                            ct_f32[:, 512 * oc:512 * oc + 512],
                            chid[:, 512 * oc:512 * oc + 512], ALU.subtract)
                    bank_free[2 + oc % 4] = [e_ev]
                    ct_evs.append(e_l)
                e_c1 = P.dma("gpsimd", chs[:, 512 * st:512 * st + 512].rearrange(
                    "(c p) t -> p c t", p=128),
                    chid[:, 0:2048].rearrange("p (c t) -> p c t", t=512),
                    after=ct_evs, lane="a_ct")
                e_c2 = P.dma("gpsimd", cls[:, 512 * st:512 * st + 512].rearrange(
                    "(c p) t -> p c t", p=128),
                    clod[:, 0:2048].rearrange("p (c t) -> p c t", t=512),
                    after=ct_evs, lane="a_ct")
                spill_evs += [e_c1, e_c2]
                lnT_reader_evs += [e_c1, e_c2]
                prev_ctfree = [e_c2]
            prev_lnT_readers = lnT_reader_evs + lnT_spill_evs
            if do_kv:
                v_evs = []
                last_v_mm = None
                for rc in range(4):
                    for ot in range(2):
                        bank = p[2 + (2 * rc + ot) % 4]
                        evm = mm3_group(
                            bank[:, 0:512], ctk_hi, ctk_lo,
                            lambda fc, rc=rc: (512 * fc + 128 * rc,
                                               512 * fc + 128 * rc + 128),
                            wuv_hi, wuv_lo,
                            lambda fc, ot=ot: (1024 * fc + 512 * ot,
                                               1024 * fc + 512 * ot + 512),
                            4, bank_free[2 + (2 * rc + ot) % 4],
                            after_fc=lambda fc: [ct_evs[fc]])
                        e_v = E("vector", [evm] + prev_vfree, "tensor_tensor",
                                v_f32[:, 1024 * rc + 512 * ot:1024 * rc + 512 * ot + 512],
                                bank[:, 0:512],
                                bvb_bc[:, 512 * ot:512 * ot + 512], ALU.add)
                        bank_free[2 + (2 * rc + ot) % 4] = [e_v]
                        last_v_mm = evm
                        v_evs.append(e_v)
                vsp = []
                for rc in range(4):
                    vdst = vp_s[512 * st + 128 * rc:512 * st + 128 * rc + 128,
                                :].rearrange("p (g u) -> p g u", u=65)[:, :, 0:64]
                    vsrc = v_f32[:, 1024 * rc:1024 * rc + 1024].rearrange(
                        "p (g w) -> p g w", w=64)
                    vsp.append(P.dma("gpsimd", vdst, vsrc, after=v_evs,
                                     lane="a_v"))
                prev_vfree = vsp[-1:]
                prev_ctfree = prev_ctfree + [last_v_mm]
                spill_evs += vsp
            prev_xfree = list(ev_app)
        return spill_evs

    a1k_done = emit_a1(x_all, 8, lnTk_hi, lnTk_lo, ctkv_hi, ctkv_lo,
                       None, None, 4, True, [ev_wprep])
    a1q_done = emit_a1(x_own, 4, None, None, ctq_hi, ctq_lo,
                       wdq_hi, wdq_lo, 0, False, list(a1k_done),
                       spill_lnT=False)
    P.barrier()

    def emit_a2(ntiles, ct_h_s, ct_l_s, use_lnT, lnh_s, lnl_s,
                wA_hi, wA_lo, boffA, wB_hi, wB_lo, boffB, nfcB,
                cos_d, sin_d, loc_d, los_d, dst_hi_s, dst_lo_s, nrm,
                ctbuf_hi, ctbuf_lo, lnbuf_hi, lnbuf_lo,
                pc_f32, pr_f32, cos_sb, sin_sb, tmpA, tmpB, tmpLo, sqbuf,
                out_hi, out_lo, start_evs, tmpG1=None, tmpG2=None):
        prev_ct_readers = list(start_evs)
        prev_ln_readers = list(start_evs)
        prev_pc_readers = list(start_evs)
        prev_out_readers = list(start_evs)
        bank_free = {i: list(start_evs) for i in range(2, 6)}
        nrm_ev = [None]
        for st in range(ntiles):
            loads = []
            e1 = P.dma("sync", ctbuf_hi.rearrange("p (c t) -> p c t", t=512),
                       ct_h_s[:, 512 * st:512 * st + 512].rearrange(
                           "(c p) t -> p c t", p=128), after=prev_ct_readers)
            e2 = P.dma("sync", ctbuf_lo.rearrange("p (c t) -> p c t", t=512),
                       ct_l_s[:, 512 * st:512 * st + 512].rearrange(
                           "(c p) t -> p c t", p=128), after=prev_ct_readers)
            loads += [e1, e2]
            if use_lnT:
                e3 = P.dma("sync", lnbuf_hi.rearrange("p (c t) -> p c t", t=512),
                           lnh_s[:, 512 * st:512 * st + 512].rearrange(
                               "(c p) t -> p c t", p=128), after=prev_ln_readers)
                e4 = P.dma("sync", lnbuf_lo.rearrange("p (c t) -> p c t", t=512),
                           lnl_s[:, 512 * st:512 * st + 512].rearrange(
                               "(c p) t -> p c t", p=128), after=prev_ln_readers)
                loads += [e3, e4]
            ec = P.dma("sync", cos_sb.rearrange("p (c t) -> p c t", t=512),
                       cos_d[:, 512 * st:512 * st + 512].rearrange(
                           "(c p) t -> p c t", p=128), after=prev_pc_readers)
            es = P.dma("sync", sin_sb.rearrange("p (c t) -> p c t", t=512),
                       sin_d[:, 512 * st:512 * st + 512].rearrange(
                           "(c p) t -> p c t", p=128), after=prev_pc_readers)
            el1 = P.dma("sync", tmpLo[0:1, 0:512],
                        loc_d[0:1, 512 * st:512 * st + 512], after=prev_pc_readers)
            el2 = P.dma("sync", tmpLo[0:1, 512:1024],
                        los_d[0:1, 512 * st:512 * st + 512], after=prev_pc_readers)
            proj_evs = {"c": [None] * 8, "r": [None] * 8}
            reader_evs = []
            rope_evs = []
            order = []
            for c4 in range(4):
                order += [("c", c4), ("c", c4 + 4), ("r", c4), ("r", c4 + 4)]
            specs = {"c": (wA_hi, wA_lo, boffA, ctbuf_hi, ctbuf_lo, 4, pc_f32),
                     "r": (wB_hi, wB_lo, boffB,
                           lnbuf_hi if use_lnT else ctbuf_hi,
                           lnbuf_lo if use_lnT else ctbuf_lo, nfcB, pr_f32)}
            for gi, (which, oc) in enumerate(order):
                whi, wlo, boff, rhi, rlo, nfc, dstb = specs[which]
                bank = p[2 + gi % 4]
                evm = mm3_group(
                    bank[:, 0:512], whi, wlo,
                    lambda fc, oc=oc: (1024 * fc + 128 * oc,
                                       1024 * fc + 128 * oc + 128),
                    rhi, rlo,
                    lambda fc: (512 * fc, 512 * fc + 512),
                    nfc, loads + bank_free[2 + gi % 4])
                e_ev = E("scalar", [evm] + (prev_pc_readers if st > 0 else []),
                         "activation",
                         out=dstb[:, 512 * oc:512 * oc + 512],
                         in_=bank[:, 0:512], func=ACT.Identity,
                         bias=biasc[:, boff + oc:boff + oc + 1])
                bank_free[2 + gi % 4] = [e_ev]
                reader_evs.append(evm)
                proj_evs[which][oc] = e_ev
                if which == "r" and oc >= 4:
                    c4 = oc - 4
                    eng2 = "vector" if c4 % 2 == 0 else "gpsimd"
                    x0 = pr_f32[:, 512 * c4:512 * c4 + 512]
                    x1 = pr_f32[:, 2048 + 512 * c4:2048 + 512 * c4 + 512]
                    cc = cos_sb[:, 512 * c4:512 * c4 + 512]
                    sc = sin_sb[:, 512 * c4:512 * c4 + 512]
                    q0 = pc_f32[:, 512 * c4:512 * c4 + 512]
                    q1 = pc_f32[:, 2048 + 512 * c4:2048 + 512 * c4 + 512]
                    tA = tmpA if c4 % 2 == 0 else tmpG1
                    tB = tmpB if c4 % 2 == 0 else tmpG2
                    dep = [proj_evs["c"][c4], proj_evs["c"][c4 + 4],
                           proj_evs["r"][c4], proj_evs["r"][c4 + 4], ec, es]
                    ea = E(eng2, dep, "tensor_tensor", tA, x0, cc, ALU.mult)
                    eb = E(eng2, [ea], "tensor_tensor", tB, x1, sc, ALU.mult)
                    e0 = E(eng2, [ea, eb], "tensor_tensor", tA, tA, tB,
                           ALU.subtract)
                    e0b = E(eng2, [e0], "tensor_tensor", q0, q0, tA, ALU.add)
                    e2a = E(eng2, [e0], "tensor_tensor", tB, x1, cc, ALU.mult)
                    e2b = E(eng2, [e0b], "tensor_tensor", tA, x0, sc, ALU.mult)
                    e1c = E(eng2, [e2a, e2b], "tensor_tensor", tB, tB, tA,
                            ALU.add)
                    e1d = E(eng2, [e1c], "tensor_tensor", q1, q1, tB, ALU.add)
                    rope_evs += [e0b, e1d]
            prev_ct_readers = [reader_evs[-1]]
            prev_ln_readers = [reader_evs[-1]]
            x0r = pr_f32[0:1, 0:512]
            x1r = pr_f32[0:1, 2048:2560]
            lc = tmpLo[0:1, 0:512]
            ls = tmpLo[0:1, 512:1024]
            ca = E("vector", [el1, el2] + rope_evs, "tensor_tensor",
                   tmpA[0:1, :], x0r, lc, ALU.mult)
            cb = E("vector", [ca], "tensor_tensor", tmpB[0:1, :], x1r, ls, ALU.mult)
            cc1 = E("vector", [ca, cb], "tensor_tensor", tmpA[0:1, :], tmpA[0:1, :],
                    tmpB[0:1, :], ALU.subtract)
            cd = E("vector", [cc1], "tensor_tensor", pc_f32[0:1, 0:512],
                   pc_f32[0:1, 0:512], tmpA[0:1, :], ALU.add)
            ce = E("vector", [cc1, cd], "tensor_tensor", tmpA[0:1, :], x1r, lc,
                   ALU.mult)
            cf = E("vector", [ce], "tensor_tensor", tmpB[0:1, :], x0r, ls, ALU.mult)
            cg = E("vector", [ce, cf], "tensor_tensor", tmpA[0:1, :], tmpA[0:1, :],
                   tmpB[0:1, :], ALU.add)
            ch2 = E("vector", [cg], "tensor_tensor", pc_f32[0:1, 2048:2560],
                    pc_f32[0:1, 2048:2560], tmpA[0:1, :], ALU.add)
            asm_evs = rope_evs + [cd, ch2]
            e_hi = E("vector", asm_evs + prev_out_readers, "tensor_copy",
                     out_hi[:, 0:4096], pc_f32[:, 0:4096])
            e_lo = E("vector", [e_hi], "tensor_tensor", out_lo[:, 0:4096],
                     pc_f32[:, 0:4096], out_hi[:, 0:4096], ALU.subtract)
            prev_pc_readers = [e_hi, e_lo]
            sq_evs = []
            nev = None
            for fc in range(8):
                half = sqbuf[:, 512 * (fc % 2):512 * (fc % 2) + 512]
                e_sq = E("scalar", [e_hi] + ([sq_evs[-2]] if fc >= 2 else [])
                         + _sq_chain[:],
                         "activation", out=half,
                         in_=out_hi[:, 512 * fc:512 * fc + 512], func=ACT.Square)
                _sq_chain[:] = [e_sq]
                nev = E("tensor", [e_sq] + ([nrm_ev[0]] if fc == 0 and nrm_ev[0] else []),
                        "matmul", p[6][0:NH, 0:512],
                        ind_sb[:, NH * fc:NH * fc + NH], half,
                        start=(fc == 0), stop=(fc == 7))
                sq_evs.append(nev)
            if st == 0:
                e_n = E("vector", [nev], "tensor_copy", nrm[0:NH, 0:512],
                        p[6][0:NH, 0:512])
            else:
                e_n = E("vector", [nev], "tensor_tensor", nrm[0:NH, 0:512],
                        nrm[0:NH, 0:512], p[6][0:NH, 0:512], ALU.max)
            nrm_ev[0] = e_n
            sp = []
            for buf, dscr in ((out_hi, dst_hi_s), (out_lo, dst_lo_s)):
                dst = dscr[:, 512 * st:512 * st + 512].rearrange(
                    "(c p) t -> p c t", p=128)
                srcv = buf[:, 0:4096].rearrange("p (c t) -> p c t", t=512)
                sp.append(P.dma("gpsimd", dst, srcv, after=[e_hi, e_lo],
                                lane="a_qk"))
            prev_out_readers = sp[-1:] + [sq_evs[-1]]
        return sp + [nrm_ev[0]]

    af.reset(PERSIST_F)
    ab.reset(PERSIST_B)
    wstage2 = af.alloc(4096)
    pc_f32 = af.alloc(4096)
    pr_f32 = af.alloc(4096)
    cos_sb = af.alloc(2048)
    sin_sb = af.alloc(2048)
    tmpA2 = af.alloc(512)
    tmpB2 = af.alloc(512)
    tmpG1a = af.alloc(512)
    tmpG2a = af.alloc(512)
    tmpLo2 = af.alloc(1024)
    sqbuf2 = af.alloc(1024)

    wA_hi = ab.alloc(4096); wA_lo = ab.alloc(4096)
    wB_hi = ab.alloc(8192); wB_lo = ab.alloc(8192)
    ctb_hi = ab.alloc(2048); ctb_lo = ab.alloc(2048)
    lnb_hi = ab.alloc(4096); lnb_lo = ab.alloc(4096)
    o_hi = ab.alloc(4096); o_lo = ab.alloc(4096)

    ev = prep_w_hilo(wuq, (0, 4), 128, wstage2, wA_hi, wA_lo, list(a1q_done))
    ev = prep_w_hilo(wqr, (0, 4), 128, wstage2, wB_hi, wB_lo, [ev])
    a2a_done = emit_a2(4, ctq_hi, ctq_lo, False, None, None,
                       wA_hi, wA_lo, 8, wB_hi, wB_lo, 16, 4,
                       cosq_d, sinq_d, locq_d, losq_d, qs_hi, qs_lo, nrm_q,
                       ctb_hi, ctb_lo, lnb_hi, lnb_lo,
                       pc_f32, pr_f32, cos_sb, sin_sb, tmpA2, tmpB2, tmpLo2,
                       sqbuf2, o_hi, o_lo, [ev], tmpG1=tmpG1a, tmpG2=tmpG2a)

    ev = prep_w_hilo(wuk, (0, 4), 128, wstage2, wA_hi, wA_lo, list(a2a_done))
    ev = prep_w_hilo(wkr, (0, 4), 128, wstage2, wB_hi, wB_lo, [ev])
    ev = prep_w_hilo(wkr, (4, 4), 128, wstage2,
                     wB_hi[:, 4096:8192], wB_lo[:, 4096:8192], [ev])
    a2b_done = emit_a2(8, ctkv_hi, ctkv_lo, True, lnTk_hi, lnTk_lo,
                       wA_hi, wA_lo, 24, wB_hi, wB_lo, 32, 8,
                       cosk_d, sink_d, lock_d, losk_d, ks_hi, ks_lo, nrm_k,
                       ctb_hi, ctb_lo, lnb_hi, lnb_lo,
                       pc_f32, pr_f32, cos_sb, sin_sb, tmpA2, tmpB2, tmpLo2,
                       sqbuf2, o_hi, o_lo, [ev], tmpG1=tmpG1a, tmpG2=tmpG2a)
    P.barrier()

    e_mq = E("vector", a2b_done, "reduce_max", cols[0:NH, 2:3],
             nrm_q[0:NH, 0:512], axis=AX.X)
    e_mk = E("vector", a2b_done, "reduce_max", cols[0:NH, 3:4],
             nrm_k[0:NH, 0:512], axis=AX.X)
    e_sq1 = E("scalar", [e_mq], "activation", out=cols[0:NH, 4:5],
              in_=cols[0:NH, 2:3], func=ACT.Sqrt)
    e_sq2 = E("scalar", [e_mk], "activation", out=cols[0:NH, 5:6],
              in_=cols[0:NH, 3:4], func=ACT.Sqrt)
    e_bnd = E("vector", [e_sq1, e_sq2], "tensor_tensor", cols[0:NH, 6:7],
              cols[0:NH, 4:5], cols[0:NH, 5:6], ALU.mult)
    e_mneg = E("gpsimd", [e_bnd], "tensor_scalar_mul", cols[0:NH, 7:8],
               cols[0:NH, 6:7], -NORM_SLACK * SCALE_INV)
    e_diag = E("vector", [e_mneg], "tensor_scalar", cols[0:NH, 8:24],
               id16_sb[0:NH, :], cols[0:NH, 7:8], None, ALU.mult)
    e_mmm = E("tensor", [e_diag], "matmul", p[6][:, 0:16],
              ones16_sb[0:NH, :], cols[:, 8:24][0:NH, :], start=True, stop=True)
    e_mcols = E("vector", [e_mmm], "tensor_copy", m_cols[:, 0:16], p[6][:, 0:16])

    af.reset(PERSIST_F)
    ab.reset(PERSIST_B)
    v_sb = af.alloc(4160)
    aT = [af.alloc(512), af.alloc(512), af.alloc(512), af.alloc(512)]
    anf_b = [af.alloc(512) for _ in range(4)]
    prodb = af.alloc(512)
    dstage = af.alloc(2048)

    qh_sb = ab.alloc(2048); ql_sb = ab.alloc(2048)
    kh_sb = ab.alloc(4096); kl_sb = ab.alloc(4096)
    au_hi = ab.alloc(2048); au_lo = ab.alloc(2048)

    wtile0 = af.alloc(4096)
    qa = af.alloc(4096)
    qb = ab.alloc(4096)
    wtile = [wtile0, qa]
    rcols = af.alloc(32)
    wt_b2 = [ab.alloc(4096), ab.alloc(4096)]
    trbuf = ab.alloc(1024)

    wq_steps = []
    wq_last = [e_mcols]
    bc_users = {0: [e_mcols], 1: [e_mcols]}
    pend_wq_tr = []
    pb7_free = [e_mcols]
    trb_free = [e_mcols]
    wb_free = {0: [e_mcols], 1: [e_mcols]}
    wq_tr_done = [e_mcols]
    wq_sp_last = [e_mcols]

    def make_wquant(wdram, ntiles, ncols, nblk, wT_scr, m_col, thr_col):
        st_ = {"prev": [None], "prev_l": [None, None], "prev_tr": [None],
               "red": [], "sp": [None]}

        def mean_slice(t0, t1, start_evs):
            def go():
                for ti in range(t0, t1):
                    buf = wtile[ti % 2]
                    e_l = P.dma("sync", buf[:, 0:ncols],
                                wdram[128 * ti:128 * ti + 128, :],
                                after=(wq_last[:] if ti < t0 + 2 else []) +
                                ([st_["red"][-2]] if len(st_["red"]) >= 2 else []),
                                lane=f"wq_{wT_scr.tensor.name}{ti % 2}")
                    e_r = E("vector", [e_l], "reduce_sum",
                            rcols[:, ti % 32:ti % 32 + 1],
                            buf[:, 0:ncols], axis=AX.X, apply_absolute_value=True)
                    st_["red"].append(e_r)
                wq_last[:] = [st_["red"][-1]]
            return go

        def chain():
            def go():
                red_evs = st_["red"]
                e_acc = E("vector", red_evs, "reduce_sum", cols[:, 28:29],
                          rcols[:, 0:ntiles], axis=AX.X)
                e_ps = E("tensor", [e_acc] + bc_users[0] + bc_users[1],
                         "matmul", p[6][0:1, 0:1],
                         cols[:, 28:29], onecol_c[:, 0:1], start=True, stop=True)
                e_mn = E("vector", [e_ps], "tensor_scalar_mul", cols[0:1, 29:30],
                         p[6][0:1, 0:1], 1.0 / (ntiles * 128 * ncols))
                e_cl = E("gpsimd", [e_mn], "tensor_scalar_max", cols[0:1, 30:31],
                         cols[0:1, 29:30], 1e-5)
                e_bc = E("tensor", [e_cl], "matmul", p[6][:, 4:5],
                         onesc_sb[0:1, 0:128], cols[0:1, 30:31],
                         start=True, stop=True)
                e_m = E("scalar", [e_bc], "activation", out=m_col[:, 0:1],
                        in_=p[6][:, 4:5], func=ACT.Copy)
                e_thr = E("scalar", [e_bc], "activation", out=thr_col[:, 0:1],
                          in_=p[6][:, 4:5], func=ACT.Copy, scale=0.5)
                bc_users[0] = [e_mn, e_m, e_thr]
                bc_users[1] = [e_mn, e_m, e_thr]
                st_["prev_l"] = [list(wq_last) + [e_m, e_thr]]
                st_["prev_tr"] = [list(wq_last) + [e_m, e_thr]]
                wq_last[:] = wq_last[:] + [e_thr]
            return go

        def make_tr_group(ti, wb, e_t, g, is_last):
            def go_tr():
                tevs = []
                for j in range(8):
                    ic = 8 * g + j
                    tevs.append(E("tensor", [e_t] + (pb7_free if j == 0 else []),
                                  "transpose",
                                  pb7[:, 128 * j:128 * j + 128],
                                  wb[:, 128 * ic:128 * ic + 128],
                                  identb[:, 0:128]))
                e_ev = E("scalar", [tevs[-1]] + trb_free, "activation",
                         out=trbuf[:, 0:1024], in_=pb7[:, 0:1024],
                         func=ACT.Copy)
                pb7_free[:] = [e_ev]
                sp = None
                for j in range(8):
                    ic = 8 * g + j
                    sp = P.dma(
                        "gpsimd",
                        wT_scr[128 * ic:128 * ic + 128,
                               128 * ti:128 * ti + 128],
                        trbuf[:, 128 * j:128 * j + 128],
                        after=[e_ev] + (wq_sp_last[:] if j == 0 else []),
                        lane="wq_sp")
                wq_sp_last[:] = [sp]
                trb_free[:] = [sp]
                if is_last:
                    wb_free[ti % 2] = [tevs[-1]]
                wq_tr_done[:] = [tevs[-1], sp]
            go_tr.par = ti % 2
            return go_tr

        def quant_slice(t0, t1):
            def go():
                for ti in range(t0, t1):
                    buf = wtile[0]
                    wb = wt_b2[ti % 2]
                    e_l = P.dma("sync", buf[:, 0:ncols],
                                wdram[128 * ti:128 * ti + 128, :],
                                after=st_["prev_l"][0] + (wq_last[:] if ti == t0 else []),
                                lane=f"wq_{wT_scr.tensor.name}0")
                    e_a = E("vector", [e_l], "tensor_single_scalar",
                            qa[:, 0:ncols], buf[:, 0:ncols], thr_col[:, 0:1],
                            ALU.is_ge)
                    e_b = E("vector", [e_l, e_a], "tensor_scalar",
                            qb[:, 0:ncols], buf[:, 0:ncols], -1.0,
                            thr_col[:, 0:1], ALU.mult, ALU.is_ge)
                    while any(getattr(c, "par", None) == ti % 2
                              for c in pend_wq_tr):
                        pend_wq_tr.pop(0)()
                    e_t = E("vector", [e_a, e_b] + wb_free[ti % 2],
                            "tensor_tensor",
                            wb[:, 0:ncols], qa[:, 0:ncols], qb[:, 0:ncols],
                            ALU.subtract)
                    st_["prev_l"][0] = [e_t]
                    ng = nblk // 8
                    for g in range(ng):
                        pend_wq_tr.append(
                            make_tr_group(ti, wb, e_t, g, g == ng - 1))
                wq_last[:] = [st_["prev_l"][0][0]]
            return go

        return mean_slice, chain, quant_slice, st_

    wg_m, wg_c, wg_q, wg_st = make_wquant(wg, 32, D, 8, wgT_s, mgate_c, cols[:, 32:33])
    wu_m, wu_c, wu_q, wu_st = make_wquant(wu, 32, D, 8, wuT_s, mup_c, cols[:, 33:34])
    wd_m, wd_c, wd_q, wd_st = make_wquant(wd, 8, INNER, 32, wdT_s, mdown_c, cols[:, 34:35])
    wq_steps = (
        [wg_m(0, 16, [e_mcols]), wg_m(16, 32, [e_mcols]), wg_c()]
        + [wg_q(2 * i, 2 * i + 2) for i in range(16)]
        + [wu_m(0, 16, [e_mcols]), wu_m(16, 32, [e_mcols]), wu_c()]
        + [wu_q(2 * i, 2 * i + 2) for i in range(16)]
        + [wd_m(0, 4, [e_mcols]), wd_m(4, 8, [e_mcols]), wd_c()]
        + [wd_q(i, i + 1) for i in range(8)]
    )

    b_prev_loads = [e_mcols]
    b_prev_spills = list(a2b_done[-1:])
    attn_spills = []
    wq_slices = []
    slot_free = {s: [e_mcols] for s in range(4)}
    dstage_free = {s: [e_mcols] for s in range(4)}
    prod_free = {0: [e_mcols], 1: [e_mcols]}

    def emit_b_pair(pr, start_evs, drive=None, pop_tr=None):
        loads = []
        for sb, dscr in ((qh_sb, qs_hi), (ql_sb, qs_lo),
                         (kh_sb, ks_hi), (kl_sb, ks_lo)):
            T = dscr.shape[1]
            for hd_ in range(2):
                hh = 2 * pr + hd_
                loads.append(P.dma("sync", sb[64 * hd_:64 * hd_ + 32, 0:T],
                                   dscr[32 * hh:32 * hh + 32, :],
                                   after=start_evs))
                loads.append(P.dma("sync", sb[64 * hd_ + 32:64 * hd_ + 64, 0:T],
                                   dscr[512 + 32 * hh:512 + 32 * hh + 32, :],
                                   after=start_evs))
        vv = v_sb.rearrange("p (c w) -> p c w", w=130)
        for side in range(2):
            loads.append(P.dma(
                "sync", vv[:, :, 65 * side:65 * side + 64],
                vp_s[:, 130 * pr + 65 * side:130 * pr + 65 * side + 64].rearrange(
                    "(c p) w -> p c w", p=128), after=start_evs))
        em1 = E("vector", [loads[-1]], "memset", vv[:, :, 64:65], 1.0)
        em2 = E("vector", [loads[-1]], "memset", vv[:, :, 129:130], 1.0)
        ev_ld = loads + [em1, em2]

        exp_done = {}
        av_done = {}
        evac_done = [None, None]
        stage_done = {}
        rcp_done = {}
        pend_n = []
        last_l = {}

        def flush_norm(hd_, q0):
            slot = 2 * hd_ + (q0 % 2)
            ds = slice(512 * slot, 512 * slot + 512)
            rr = slice(64 * hd_, 64 * hd_ + 64)
            e_bcx = E("tensor", [rcp_done[slot]] + bc_users[hd_], "matmul",
                      p[6][rr, 0:512],
                      onesc_sb[0:1, 0:64], dstage[0:1, ds],
                      start=True, stop=True)
            e_nn = E("vector", [e_bcx] + prod_free[hd_], "tensor_tensor",
                     prodb[rr, 0:512], anf_b[slot][0:64, 0:512],
                     p[6][rr, 0:512], ALU.mult)
            bc_users[hd_] = [e_nn]
            au_dep = list(start_evs) if q0 == 0 else []
            e_hh = E("vector", [e_nn] + au_dep, "tensor_copy",
                     au_hi[rr, 512 * q0:512 * q0 + 512],
                     prodb[rr, 0:512])
            e_ll = E("vector", [e_hh], "tensor_tensor",
                     au_lo[rr, 512 * q0:512 * q0 + 512],
                     prodb[rr, 0:512],
                     au_hi[rr, 512 * q0:512 * q0 + 512],
                     ALU.subtract)
            prod_free[hd_] = [e_ll]
            slot_free[slot] = [e_nn]
            dstage_free[slot] = [e_bcx]
            last_l[hd_] = e_ll
        for qt in range(4):
            def emit_scores(chn):
                par = chn % 2
                for hd_ in range(2):
                    bank = p[2 * hd_ + par]
                    pe = exp_done.get((hd_, par))
                    dep = [pe] if pe is not None else list(ev_ld) + [e_mcols]
                    row0 = 64 * hd_
                    tp = (64 * hd_, 0)
                    E("tensor", dep, "matmul", bank[:, 0:512],
                      kh_sb[row0:row0 + 64, 128 * chn:128 * chn + 128],
                      qh_sb[row0:row0 + 64, 512 * qt:512 * qt + 512],
                      start=True, stop=False, tile_position=tp)
                    E("tensor", [], "matmul", bank[:, 0:512],
                      kh_sb[row0:row0 + 64, 128 * chn:128 * chn + 128],
                      ql_sb[row0:row0 + 64, 512 * qt:512 * qt + 512],
                      start=False, stop=False, tile_position=tp)
                    e3 = E("tensor", [], "matmul", bank[:, 0:512],
                          kl_sb[row0:row0 + 64, 128 * chn:128 * chn + 128],
                          qh_sb[row0:row0 + 64, 512 * qt:512 * qt + 512],
                          start=False, stop=True, tile_position=tp)
                    adep = [e3]
                    pa = av_done.get((hd_, par))
                    if pa is not None:
                        adep.append(pa)
                    e_x = E("scalar", adep, "activation",
                            out=aT[2 * hd_ + par][:, 0:512],
                            in_=bank[:, 0:512],
                            func=ACT.Exp, scale=scinv_c[:, 0:1],
                            bias=m_cols[:, 2 * pr + hd_:2 * pr + hd_ + 1])
                    exp_done[(hd_, par)] = e_x

            def emit_av(chn):
                par = chn % 2
                for hd_ in range(2):
                    avdep = [exp_done[(hd_, par)]]
                    if chn == 0 and evac_done[hd_] is not None:
                        avdep.append(evac_done[hd_])
                    e_av = E("tensor", avdep, "matmul",
                             p[4 + hd_][0:65, 0:512],
                             v_sb[:, 130 * chn + 65 * hd_:130 * chn + 65 * hd_ + 65],
                             aT[2 * hd_ + par][:, 0:512],
                             start=(chn == 0), stop=(chn == 31))
                    av_done[(hd_, par)] = e_av

            emit_scores(0)
            for chn in range(1, 32):
                emit_scores(chn)
                emit_av(chn - 1)
                if chn == 6:
                    while pend_n:
                        h0, q0 = pend_n.pop(0)
                        flush_norm(h0, q0)
                if chn == 16 and drive is not None:
                    drive()
                if chn in (8, 14, 20, 26) and pop_tr is not None:
                    pop_tr()
            emit_av(31)
            if drive is not None:
                drive()
            for hd_ in range(2):
                slot = 2 * hd_ + (qt % 2)
                e_u = E("vector", [av_done[(hd_, 0)], av_done[(hd_, 1)]]
                        + slot_free[slot],
                        "tensor_copy", anf_b[slot][0:65, 0:512],
                        p[4 + hd_][0:65, 0:512])
                stage_done[slot] = e_u
                evac_done[hd_] = e_u
                pend_n.append((hd_, qt))
            for hd_ in range(2):
                slot = 2 * hd_ + (qt % 2)
                ds = slice(512 * slot, 512 * slot + 512)
                rcp_done[slot] = E(
                    "vector", [stage_done[slot]] + dstage_free[slot],
                    "reciprocal", dstage[0:1, ds],
                    anf_b[slot][64:65, 0:512])
        while pend_n:
            h0, q0 = pend_n.pop(0)
            flush_norm(h0, q0)
        s1 = P.dma("gpsimd", attnU_hi_s[128 * pr:128 * pr + 128, :],
                   au_hi[:, 0:T_OWN], after=[last_l[0], last_l[1]],
                   lane="b_au")
        s2 = P.dma("gpsimd", attnU_lo_s[128 * pr:128 * pr + 128, :],
                   au_lo[:, 0:T_OWN], after=[last_l[0], last_l[1]],
                   lane="b_au")
        return [s1, s2, evac_done[0], evac_done[1]]

    pair_done = b_prev_loads
    wq_i = [0]

    def wq_drive():
        if wq_i[0] < len(wq_steps):
            wq_steps[wq_i[0]]()
            wq_i[0] += 1

    def wq_pop_tr():
        if pend_wq_tr:
            pend_wq_tr.pop(0)()

    for pr in range(8):
        pair_done = emit_b_pair(pr, pair_done, drive=wq_drive,
                                pop_tr=wq_pop_tr)
        attn_spills += pair_done[:2]
    while wq_i[0] < len(wq_steps) or pend_wq_tr:
        wq_pop_tr()
        wq_drive()
    P.barrier()



    P.barrier()

    af.reset(PERSIST_F)
    ab.reset(PERSIST_B)
    xqT_sb = ab.alloc(16384)
    PERSIST_B2 = ab.mark()
    projs2 = [af.alloc(1024), af.alloc(1024)]
    projb2 = [af.alloc(1024), af.alloc(1024)]
    xrow2 = [af.alloc(1024), af.alloc(1024)]
    x2row2 = [af.alloc(1024), af.alloc(1024)]
    ln2row2 = [af.alloc(1024), af.alloc(1024)]
    spacer = af.alloc(1536)
    rq = [af.alloc(1024) for _ in range(3)]

    wstage3 = af.alloc(4096)
    au_h = ab.alloc(2048)
    au_l = ab.alloc(2048)
    au_h2 = [au_h[:, 0:1024], au_h[:, 1024:2048]]
    au_l2 = [au_l[:, 0:1024], au_l[:, 1024:2048]]
    anh = ab.alloc(1024); anl = ab.alloc(1024)
    wo_hi = ab.alloc(8192); wo_lo = ab.alloc(8192)
    xq_b2 = [ab.alloc(1024), ab.alloc(1024)]

    ev_wo = prep_w_hilo(wo, (0, 4), 128, wstage3, wo_hi, wo_lo,
                        list(wq_tr_done))
    ev_wo = prep_w_hilo(wo, (4, 4), 128, wstage3,
                        wo_hi[:, 4096:8192], wo_lo[:, 4096:8192], [ev_wo])

    cn_done = attn_spills[-2:] + [ev_wo]

    def ln_single(src, after, b=36):
        e_s = E("vector", after, "reduce_sum", stats[:, b:b + 1], src, axis=AX.X)
        e_q = E("scalar", list(after) + _sq_chain[:], "activation",
                out=spacer[:, 0:1024],
                in_=src, func=ACT.Square, accum_out=stats[:, b + 1:b + 2])
        _sq_chain[:] = [e_q]
        e_mu = E("gpsimd", [e_s], "tensor_scalar_mul", stats[:, b + 2:b + 3],
                 stats[:, b:b + 1], 1.0 / 1024)
        e_t3 = E("gpsimd", [e_q], "tensor_scalar", stats[:, b + 3:b + 4],
                 stats[:, b + 1:b + 2], 1.0 / 1024, 1e-5, ALU.mult, ALU.add)
        e_m2 = E("vector", [e_mu], "tensor_tensor", stats[:, b + 4:b + 5],
                 stats[:, b + 2:b + 3], stats[:, b + 2:b + 3], ALU.mult)
        e_v = E("gpsimd", [e_t3, e_m2], "tensor_tensor", stats[:, b + 5:b + 6],
                stats[:, b + 3:b + 4], stats[:, b + 4:b + 5], ALU.subtract)
        e_sd = E("scalar", [e_v], "activation", out=stats[:, b + 6:b + 7],
                 in_=stats[:, b + 5:b + 6], func=ACT.Sqrt)
        e_r = E("vector", [e_sd], "reciprocal", stats[:, b + 7:b + 8],
                stats[:, b + 6:b + 7])
        e_sp = E("vector", _sp_chain[:], "tensor_copy",
                 spacer[:, 1024:1536], src[:, 0:512])
        _sp_chain[:] = [e_sp]
        return e_r, e_sp

    prev_au = list(cn_done)
    prev_an = list(cn_done)
    prev_xq = list(cn_done)
    prev_x2w = list(cn_done)
    prev_projw = list(cn_done)
    prev_ln2w = list(cn_done)
    c_bank_free = {(pp, oo): list(cn_done) for pp in range(2) for oo in range(2)}
    sc_free = {(pp, oo): list(cn_done) for pp in range(2) for oo in range(2)}
    c_done = []
    prev_par = {0: dict(au=list(cn_done), x2w=list(cn_done),
                        projw=list(cn_done), ln2w=list(cn_done),
                        xrow=list(cn_done)),
                1: dict(au=list(cn_done), x2w=list(cn_done),
                        projw=list(cn_done), ln2w=list(cn_done),
                        xrow=list(cn_done))}
    pend_tr = []
    prev_xq_par = {0: prev_xq, 1: prev_xq}

    def emit_xq_tr(tc2, e_qc2, xq_b_2):
        tr_evs = []
        tevs = []
        for fc in range(8):
            tevs.append(E("tensor", [e_qc2] + prev_xq, "transpose",
                          pb7[:, 128 * fc:128 * fc + 128],
                          xq_b_2[:, 128 * fc:128 * fc + 128],
                          identb[:, 0:128]))
        for fc in range(8):
            tr_evs.append(E("vector", [tevs[-1]], "tensor_copy",
                           xqT_sb[:, 2048 * fc + 128 * tc2:2048 * fc + 128 * tc2 + 128],
                           pb7[:, 128 * fc:128 * fc + 128]))
        return tr_evs

    for tc in range(16):
        par = tc % 2
        pv = prev_par[par]
        if len(pend_tr) >= 2:
            tc2, e_qc2, xq_b_2 = pend_tr.pop(0)
            tr_evs = emit_xq_tr(tc2, e_qc2, xq_b_2)
            prev_xq = tr_evs[-1:]
            prev_xq_par[tc2 % 2] = tr_evs[-1:]
            c_done = tr_evs[-1:]
        auh, aul = au_h2[par], au_l2[par]
        projb = projb2[par]
        xrow = xrow2[par]
        x2row = x2row2[par]
        ln2row = ln2row2[par]
        xq_b = xq_b2[par]
        sb = 36 + 16 * par
        cb = 36 if par == 0 else 8
        l1 = P.dma("sync", auh[:, 0:1024].rearrange("p (c t) -> p c t", t=128),
                   attnU_hi_s[:, 128 * tc:128 * tc + 128].rearrange(
                       "(c p) t -> p c t", p=128), after=pv["au"],
                   lane=f"c_au{par}")
        l2 = P.dma("sync", aul[:, 0:1024].rearrange("p (c t) -> p c t", t=128),
                   attnU_lo_s[:, 128 * tc:128 * tc + 128].rearrange(
                       "(c p) t -> p c t", p=128), after=pv["au"],
                   lane=f"c_au{par}")
        lx = P.dma("sync", xrow[:, 0:1024],
                   x_own[128 * tc:128 * tc + 128, :], after=pv["xrow"],
                   lane=f"c_x{par}")
        an_evs = [l1, l2]
        wo_evs = []
        for oc in range(2):
            bank = p[2 + 2 * par + oc] if par == 0 else p[4 + oc]
            evm = mm3_group(
                bank[:, 0:512], auh, aul,
                lambda fc: (128 * fc, 128 * fc + 128),
                wo_hi, wo_lo,
                lambda fc, oc=oc: (1024 * fc + 512 * oc, 1024 * fc + 512 * oc + 512),
                8, an_evs + c_bank_free[(par, oc)])
            e_cp = E("scalar", [evm] + sc_free[(par, oc)], "activation",
                     out=projs2[par][:, 512 * oc:512 * oc + 512],
                     in_=bank[:, 0:512], func=ACT.Copy)
            e_p = E("gpsimd", [e_cp] + pv["projw"], "tensor_tensor",
                    projb[:, 512 * oc:512 * oc + 512],
                    projs2[par][:, 512 * oc:512 * oc + 512],
                    bwo_bc[:, 512 * oc:512 * oc + 512], ALU.add)
            sc_free[(par, oc)] = [e_p]
            c_bank_free[(par, oc)] = [e_cp]
            wo_evs.append(e_p)
        pv["au"] = list(wo_evs)
        e_r1, e_sp1 = ln_single(projb[:, 0:1024], wo_evs, b=sb)
        e_x2 = E("vector", [e_r1, e_sp1, lx] + pv["x2w"], "tensor_scalar",
                 x2row[:, 0:1024], projb[:, 0:1024],
                 stats[:, sb + 2:sb + 3], stats[:, sb + 7:sb + 8],
                 ALU.subtract, ALU.mult)
        pv["projw"] = [e_x2]
        e_x2b = E("vector", [e_x2], "tensor_tensor", x2row[:, 0:1024],
                  x2row[:, 0:1024], xrow[:, 0:1024], ALU.add)
        pv["xrow"] = [e_x2b]
        s_x2 = P.dma("gpsimd", x2_s[128 * tc:128 * tc + 128, :],
                     x2row[:, 0:1024], after=[e_x2b], lane=f"c_x2{par}")
        e_r2, e_sp2 = ln_single(x2row[:, 0:1024], [e_x2b], b=sb)
        e_ln2 = E("vector", [e_r2, e_sp2] + pv["ln2w"], "tensor_scalar",
                  ln2row[:, 0:1024], x2row[:, 0:1024],
                  stats[:, sb + 2:sb + 3], stats[:, sb + 7:sb + 8],
                  ALU.subtract, ALU.mult)
        e_mx = E("vector", [e_ln2], "reduce_max", cols[:, cb:cb + 1],
                 ln2row[:, 0:1024], axis=AX.X, apply_absolute_value=True)
        e_mc = E("gpsimd", [e_mx], "tensor_scalar_max", cols[:, cb + 1:cb + 2],
                 cols[:, cb:cb + 1], 1e-5)
        e_rc = E("vector", [e_mc], "reciprocal", cols[:, cb + 2:cb + 3],
                 cols[:, cb + 1:cb + 2])
        e_sx = E("gpsimd", [e_rc], "tensor_scalar_mul",
                 sx_all[:, tc:tc + 1], cols[:, cb + 2:cb + 3], 127.0)
        e_e1 = E("gpsimd", [e_mc], "tensor_scalar_mul",
                 ev1_all[:, tc:tc + 1], cols[:, cb + 1:cb + 2], 1.0 / 127.0)
        e_sg = E("vector", [e_e1], "tensor_tensor", sg_all[:, tc:tc + 1],
                 ev1_all[:, tc:tc + 1], mgate_c[:, 0:1], ALU.mult)
        e_su = E("vector", [e_e1], "tensor_tensor", su_all[:, tc:tc + 1],
                 ev1_all[:, tc:tc + 1], mup_c[:, 0:1], ALU.mult)
        e_u1 = E("vector", [e_sx, e_ln2], "tensor_scalar", rq[0][:, 0:1024],
                 ln2row[:, 0:1024], sx_all[:, tc:tc + 1], C1, ALU.mult, ALU.add)
        e_u2 = E("vector", [e_u1], "tensor_scalar", rq[1][:, 0:1024],
                 ln2row[:, 0:1024], sx_all[:, tc:tc + 1], C2, ALU.mult, ALU.add)
        e_d2 = E("vector", [e_u2], "tensor_single_scalar", rq[1][:, 0:1024],
                 rq[1][:, 0:1024], C2, ALU.subtract)
        e_tq = E("vector", [e_d2], "scalar_tensor_tensor", rq[0][:, 0:1024],
                 rq[0][:, 0:1024], C1, rq[1][:, 0:1024], ALU.subtract, ALU.subtract)
        e_t2 = E("vector", [ev_zm, e_tq], "tensor_single_scalar", rq[0][:, 0:1024],
                 rq[0][:, 0:1024], zmask_c[:, 0:1], ALU.mult)
        e_qq = E("vector", prev_xq_par[par] + [e_t2], "tensor_tensor",
                 xq_b[:, 0:1024],
                 rq[0][:, 0:1024], rq[1][:, 0:1024], ALU.add)
        pv["x2w"] = [s_x2, e_ln2]
        pv["ln2w"] = [e_u2, e_mx]
        pend_tr.append((tc, e_qq, xq_b))
    while pend_tr:
        tc2, e_qc2, xq_b_2 = pend_tr.pop(0)
        tr_evs = emit_xq_tr(tc2, e_qc2, xq_b_2)
        prev_xq = tr_evs[-1:]
        c_done = [s_x2] + tr_evs[-1:]
    P.barrier()

    af.reset(PERSIST_F)
    ab.reset(PERSIST_B2)
    y_row2 = [af.alloc(4096), af.alloc(4096)]
    gbuf = af.alloc(512)
    ubuf = af.alloc(512)
    rqa = af.alloc(2048)
    rqb = af.alloc(2048)
    rga = af.alloc(2048)
    rgb = af.alloc(2048)

    xqT2 = xqT_sb
    wg_sb = ab.alloc(8192)
    wu_sb = ab.alloc(8192)
    yq_b2 = [ab.alloc(4096), ab.alloc(4096)]
    strip = ab.alloc(4096)

    d1_done = []
    pend_d1 = []
    prev_yq = {0: list(c_done), 1: list(c_done)}
    prev_w = {0: list(c_done), 1: list(c_done)}
    prev_y = {0: list(c_done), 1: list(c_done)}
    d1_rq_chain = list(c_done)
    d1_rg_chain = list(c_done)
    prev_strip = list(c_done)
    prev_gu = list(c_done)
    d_bank_free = {i: list(c_done) for i in range(4)}
    for tcp in range(8):
        tcs = (2 * tcp, 2 * tcp + 1)
        ot_evs = {0: [], 1: []}
        for ot in range(8):
            par = ot % 2
            lg = P.dma("sync", wg_sb[:, 4096 * par:4096 * par + 4096].rearrange(
                "p (c t) -> p c t", t=512),
                wgT_s[:, 512 * ot:512 * ot + 512].rearrange(
                    "(c p) t -> p c t", p=128), after=prev_w[par],
                lane=f"wg{par}")
            lu = P.dma("sync", wu_sb[:, 4096 * par:4096 * par + 4096].rearrange(
                "p (c t) -> p c t", t=512),
                wuT_s[:, 512 * ot:512 * ot + 512].rearrange(
                    "(c p) t -> p c t", p=128), after=prev_w[par],
                lane=f"wu{par}")
            last_mm = []
            for ti, tc in enumerate(tcs):
                evg = None
                evu = None
                for fc in range(8):
                    evg = E("tensor", ([lg] + d_bank_free[par]) if fc == 0 else [],
                            "matmul", p[par][:, 0:512],
                            xqT2[:, 2048 * fc + 128 * tc:2048 * fc + 128 * tc + 128],
                            wg_sb[:, 4096 * par + 512 * fc:4096 * par + 512 * fc + 512],
                            start=(fc == 0), stop=(fc == 7))
                for fc in range(8):
                    evu = E("tensor", ([lu] + d_bank_free[2 + par]) if fc == 0 else [],
                            "matmul", p[2 + par][:, 0:512],
                            xqT2[:, 2048 * fc + 128 * tc:2048 * fc + 128 * tc + 128],
                            wu_sb[:, 4096 * par + 512 * fc:4096 * par + 512 * fc + 512],
                            start=(fc == 0), stop=(fc == 7))
                e_g = E("scalar", [evg] + prev_gu, "activation", out=gbuf[:, 0:512],
                        in_=p[par][:, 0:512], func=ACT.Silu,
                        scale=sg_all[:, tc:tc + 1])
                e_u = E("scalar", [evu], "activation", out=ubuf[:, 0:512],
                        in_=p[2 + par][:, 0:512], func=ACT.Copy,
                        scale=su_all[:, tc:tc + 1])
                e_y = E("vector", [e_g, e_u] + (prev_y[ti] if ot == 0 else []),
                        "tensor_tensor",
                        y_row2[ti][:, 512 * ot:512 * ot + 512],
                        gbuf[:, 0:512], ubuf[:, 0:512], ALU.mult)
                prev_gu = [e_y]
                d_bank_free[par] = [e_g, e_y]
                d_bank_free[2 + par] = [e_u, e_y]
                last_mm += [evg, evu]
                ot_evs[ti].append(e_y)
            prev_w[par] = last_mm
        while pend_d1:
            tcd, q_evs_d, yqd = pend_d1.pop(0)
            st_evs = []
            for g in range(4):
                tevs = []
                for j in range(8):
                    ic = 8 * g + j
                    tevs.append(E("tensor",
                                  q_evs_d + prev_strip + (st_evs[-1:] if g else []),
                                  "transpose",
                                  pb7[:, 128 * j:128 * j + 128],
                                  yqd[:, 128 * ic:128 * ic + 128],
                                  identb[:, 0:128]))
                st_evs.append(E("vector", [tevs[-1]], "tensor_copy",
                                strip[:, 1024 * g:1024 * g + 1024],
                                pb7[:, 0:1024]))
            s_st = P.dma("gpsimd", xq2T_s[128 * tcd:128 * tcd + 128, :],
                         strip[:, 0:4096], after=st_evs, lane="d_st")
            prev_strip = [s_st]
            prev_yq[tcd % 2] = [st_evs[-1]]
            d1_done = [s_st]
        for ti, tc in enumerate(tcs):
            y_row = y_row2[ti]
            yq_b = yq_b2[ti]
            e_mx = E("vector", ot_evs[ti], "reduce_max", cols[:, 40:41],
                     y_row[:, 0:4096], axis=AX.X, apply_absolute_value=True)
            e_mc = E("gpsimd", [e_mx], "tensor_scalar_max", cols[:, 41:42],
                     cols[:, 40:41], 1e-5)
            e_rc = E("vector", [e_mc], "reciprocal", cols[:, 42:43], cols[:, 41:42])
            e_sx = E("gpsimd", [e_rc], "tensor_scalar_mul", cols[:, 43:44],
                     cols[:, 42:43], 127.0)
            e_sd = E("gpsimd", [e_mc], "tensor_scalar_mul", cols[:, 44:45],
                     cols[:, 41:42], 1.0 / 127.0)
            e_sd2 = E("vector", [e_sd], "tensor_tensor", sd_all[:, tc:tc + 1],
                      cols[:, 44:45], mdown_c[:, 0:1], ALU.mult)
            ysl0 = y_row[:, 0:2048]
            eh1 = E("vector", [e_sx] + d1_rq_chain, "tensor_scalar",
                    rqa[:, 0:2048], ysl0, cols[:, 43:44], C1, ALU.mult, ALU.add)
            eh2 = E("vector", [eh1], "tensor_scalar", rqb[:, 0:2048],
                    ysl0, cols[:, 43:44], C2, ALU.mult, ALU.add)
            eh3 = E("vector", [eh2], "tensor_single_scalar", rqb[:, 0:2048],
                    rqb[:, 0:2048], C2, ALU.subtract)
            eh4 = E("vector", [eh3], "scalar_tensor_tensor", rqa[:, 0:2048],
                    rqa[:, 0:2048], C1, rqb[:, 0:2048], ALU.subtract,
                    ALU.subtract)
            eh5 = E("vector", [ev_zm, eh4], "tensor_single_scalar",
                    rqa[:, 0:2048], rqa[:, 0:2048], zmask_c[:, 0:1], ALU.mult)
            eh6 = E("vector", prev_yq[ti] + [eh5], "tensor_tensor",
                    yq_b[:, 0:2048], rqa[:, 0:2048], rqb[:, 0:2048], ALU.add)
            d1_rq_chain[:] = [eh6]
            ysl1 = y_row[:, 2048:4096]
            gh1 = E("vector", [eh2] + d1_rg_chain, "tensor_scalar",
                    rga[:, 0:2048], ysl1, cols[:, 43:44], C1, ALU.mult, ALU.add)
            gh2 = E("vector", [gh1], "tensor_scalar", rgb[:, 0:2048],
                    ysl1, cols[:, 43:44], C2, ALU.mult, ALU.add)
            gh3 = E("vector", [gh2], "tensor_single_scalar", rgb[:, 0:2048],
                    rgb[:, 0:2048], C2, ALU.subtract)
            gh4 = E("vector", [gh3], "scalar_tensor_tensor", rga[:, 0:2048],
                    rga[:, 0:2048], C1, rgb[:, 0:2048], ALU.subtract,
                    ALU.subtract)
            gh5 = E("vector", [ev_zm, gh4], "tensor_single_scalar",
                    rga[:, 0:2048], rga[:, 0:2048], zmask_c[:, 0:1], ALU.mult)
            gh6 = E("vector", prev_yq[ti] + [gh5], "tensor_tensor",
                    yq_b[:, 2048:4096], rga[:, 0:2048], rgb[:, 0:2048], ALU.add)
            d1_rg_chain[:] = [gh6]
            q_evs = [eh6, gh6]
            prev_y[ti] = q_evs
            pend_d1.append((tc, list(q_evs), yq_b))
    while pend_d1:
        tcd, q_evs_d, yqd = pend_d1.pop(0)
        st_evs = []
        for g in range(4):
            tevs = []
            for j in range(8):
                ic = 8 * g + j
                tevs.append(E("tensor",
                              q_evs_d + prev_strip + (st_evs[-1:] if g else []),
                              "transpose",
                              pb7[:, 128 * j:128 * j + 128],
                              yqd[:, 128 * ic:128 * ic + 128],
                              identb[:, 0:128]))
            st_evs.append(E("vector", [tevs[-1]], "tensor_copy",
                            strip[:, 1024 * g:1024 * g + 1024], pb7[:, 0:1024]))
        s_st = P.dma("gpsimd", xq2T_s[128 * tcd:128 * tcd + 128, :],
                     strip[:, 0:4096], after=st_evs, lane="d_st")
        prev_strip = [s_st]
        d1_done = [s_st]
    P.barrier()

    af.reset(PERSIST_F)
    ab.reset(PERSIST_B)
    dnrow2 = [af.alloc(1024), af.alloc(1024)]
    x2b2 = [af.alloc(1024), af.alloc(1024)]
    outrow2 = [af.alloc(1024), af.alloc(1024)]
    spacer3 = af.alloc(1536)

    wd_sb = ab.alloc(32768)
    strip22 = [ab.alloc(4096), ab.alloc(4096)]

    e_wd = P.dma("sync", wd_sb[:, 0:32768].rearrange("p (c t) -> p c t", t=1024),
                 wdT_s[:, :].rearrange("(c p) t -> p c t", p=128), after=d1_done)

    spacer = spacer3
    d2p = {pp: dict(str2=[e_wd], out=[e_wd], dnr=[e_wd], so=[e_wd])
           for pp in range(2)}
    d2_bank = {(pp, oo): [e_wd] for pp in range(2) for oo in range(2)}
    out_dmas = []
    for tc in range(16):
        par = tc % 2
        pv2 = d2p[par]
        strip2 = strip22[par]
        dnrow = dnrow2[par]
        x2b = x2b2[par]
        outrow = outrow2[par]
        sb2 = 36 + 16 * par
        l_st = P.dma("sync", strip2[:, 0:4096],
                     xq2T_s[128 * tc:128 * tc + 128, :], after=pv2["str2"],
                     lane=f"d2s{par}")
        lx2 = P.dma("sync", x2b[:, 0:1024],
                    x2_s[128 * tc:128 * tc + 128, :], after=pv2["out"],
                    lane=f"d2x{par}")
        dn_evs = []
        for oc in range(2):
            bank = p[2 + oc] if par == 0 else p[4 + oc]
            evm = None
            for ic in range(32):
                evm = E("tensor",
                        ([l_st, e_wd] + d2_bank[(par, oc)]) if ic == 0 else [],
                        "matmul", bank[:, 0:512],
                        strip2[:, 128 * ic:128 * ic + 128],
                        wd_sb[:, 1024 * ic + 512 * oc:1024 * ic + 512 * oc + 512],
                        start=(ic == 0), stop=(ic == 31))
            e_dn = E("scalar", [evm], "activation",
                     out=dnrow[:, 512 * oc:512 * oc + 512],
                     in_=bank[:, 0:512], func=ACT.Copy,
                     scale=sd_all[:, tc:tc + 1])
            d2_bank[(par, oc)] = [e_dn] + pv2["dnr"]
            dn_evs.append(e_dn)
        e_r3, e_sp3 = ln_single(dnrow[:, 0:1024], dn_evs, b=sb2)
        e_o = E("vector", [e_r3, e_sp3, lx2] + pv2["so"], "tensor_scalar",
                outrow[:, 0:1024], dnrow[:, 0:1024],
                stats[:, sb2 + 2:sb2 + 3], stats[:, sb2 + 7:sb2 + 8],
                ALU.subtract, ALU.mult)
        e_o2 = E("vector", [e_o], "tensor_tensor", outrow[:, 0:1024],
                 outrow[:, 0:1024], x2b[:, 0:1024], ALU.add)
        s_o = P.dma("gpsimd", out_d[128 * tc:128 * tc + 128, :],
                    outrow[:, 0:1024], after=[e_o2], lane=f"out{par}")
        out_dmas.append(s_o)
        pv2["str2"] = [dn_evs[-1]]
        pv2["out"] = [e_o2]
        pv2["dnr"] = [e_o]
        pv2["so"] = [s_o]
    P.emit("gpsimd", lambda g: g.nop(), after=out_dmas)

    P.finalize()
    return nc



def _host_tables():
    pos = np.arange(S, dtype=np.float64)[:, None]
    ids = np.arange(D // 2, dtype=np.float64)
    theta = 10000.0 ** (-2.0 * ids) / (D // 2)
    r = pos * theta[None, :]
    cosT = np.cos(r).T
    sinT = np.sin(r).T
    cos32 = np.ascontiguousarray(cosT.astype(np.float32))
    sin32 = np.ascontiguousarray(sinT.astype(np.float32))
    lo_c = (cosT[0] - cos32[0].astype(np.float64)).astype(np.float32)[None, :]
    lo_s = (sinT[0] - sin32[0].astype(np.float32).astype(np.float64)
            ).astype(np.float32)[None, :]
    lo_s = (sinT[0] - sin32[0].astype(np.float64)).astype(np.float32)[None, :]
    return cos32, sin32, np.ascontiguousarray(lo_c), np.ascontiguousarray(lo_s)


def _bias_cols(b, nch):
    return np.ascontiguousarray(np.asarray(b, np.float32).reshape(nch, 128).T)


_NC_CACHE = [None]


def _prep(inp):
    f32 = np.float32
    X = np.ascontiguousarray(np.asarray(inp["L0_attn_x"], f32))
    perm = np.concatenate([np.arange(0, D, 2), np.arange(1, D, 2)])

    w = {k: np.ascontiguousarray(np.asarray(v, f32)) for k, v in inp.items()
         if k != "L0_attn_x"}
    wuq_p = np.ascontiguousarray(w["L0_attn_Wu_q_w"][:, perm])
    wqr_p = np.ascontiguousarray(w["L0_attn_Wq_r_w"][:, perm])
    wuk_p = np.ascontiguousarray(w["L0_attn_Wu_k_w"][:, perm])
    wkr_p = np.ascontiguousarray(w["L0_attn_Wk_r_w"][:, perm])
    buq_p = w["L0_attn_Wu_q_b"][perm]
    bqr_p = w["L0_attn_Wq_r_b"][perm]
    buk_p = w["L0_attn_Wu_k_b"][perm]
    bkr_p = w["L0_attn_Wk_r_b"][perm]

    cos32, sin32, lo_c, lo_s = _host_tables()

    ind = np.zeros((D, NH), f32)
    for r_ in range(D):
        hh = (r_ // 32) if r_ < 512 else ((r_ - 512) // 32)
        ind[r_, hh] = 1.0
    sel = np.zeros((NH, 1024), f32)
    for j in range(1024):
        sel[2 * (j // 128) + (1 if (j % 128) >= 64 else 0), j] = 1.0

    common = {
        "wdq": w["L0_attn_Wd_q_w"], "wdkv": w["L0_attn_Wd_kv_w"],
        "wuq": wuq_p, "wqr": wqr_p, "wuk": wuk_p, "wkr": wkr_p,
        "wuv": w["L0_attn_Wu_v_w"], "wo": w["L0_attn_Wo_w"],
        "wg": w["L0_ffn_w_gate"], "wu": w["L0_ffn_w_up"],
        "wd": w["L0_ffn_w_down"],
        "bdq_c": _bias_cols(w["L0_attn_Wd_q_b"], 4),
        "bdkv_c": _bias_cols(w["L0_attn_Wd_kv_b"], 4),
        "buq_c": _bias_cols(buq_p, 8),
        "bqr_c": _bias_cols(bqr_p, 8),
        "buk_c": _bias_cols(buk_p, 8),
        "bkr_c": _bias_cols(bkr_p, 8),
        "bvb_r": np.ascontiguousarray(w["L0_attn_Wu_v_b"][None, :]),
        "bwo_r": np.ascontiguousarray(w["L0_attn_Wo_b"][None, :]),
        "identf": np.eye(128, dtype=f32),
        "identb": np.eye(128, dtype=ml_dtypes.bfloat16),
        "ind_norm": ind, "sel_pairs": sel,
        "id16": np.eye(NH, dtype=f32),
        "ones16": np.ones((NH, 128), f32),
        "onescol": np.ones((1, 128), f32),
        "cos_k": cos32, "sin_k": sin32,
        "lo_ck": lo_c, "lo_sk": lo_s,
    }

    return common, X, cos32, sin32, lo_c, lo_s


def _in_maps(inp):
    common, X, cos32, sin32, lo_c, lo_s = _prep(inp)
    in_maps = []
    for c in range(8):
        b, hf = c // 2, c % 2
        m = dict(common)
        m["x_all"] = np.ascontiguousarray(X[b])
        m["x_own"] = np.ascontiguousarray(X[b, 2048 * hf:2048 * hf + 2048])
        m["cos_q"] = np.ascontiguousarray(cos32[:, 2048 * hf:2048 * hf + 2048])
        m["sin_q"] = np.ascontiguousarray(sin32[:, 2048 * hf:2048 * hf + 2048])
        m["lo_cq"] = np.ascontiguousarray(lo_c[:, 2048 * hf:2048 * hf + 2048])
        m["lo_sq"] = np.ascontiguousarray(lo_s[:, 2048 * hf:2048 * hf + 2048])
        in_maps.append(m)
    return in_maps


def kernel(**inp):
    in_maps = _in_maps(inp)
    if _NC_CACHE[0] is None:
        _NC_CACHE[0] = build_nc()
    nc = _NC_CACHE[0]
    res = run_bass_kernel_spmd(nc, in_maps, core_ids=list(range(8)))

    out = np.empty((4, S, D), np.float32)
    for c in range(8):
        b, hf = c // 2, c % 2
        out[b, 2048 * hf:2048 * hf + 2048] = np.asarray(res.results[c]["out"],
                                                        np.float32)
    return out
